# version 02
# baseline (speedup 1.0000x reference)
import math
import numpy as np
from contextlib import ExitStack

import concourse.bass as bass
import concourse.mybir as mybir
from concourse.bass_utils import run_bass_kernel_spmd

F32 = mybir.dt.float32
BF16 = mybir.dt.bfloat16
AF = mybir.ActivationFunctionType
ALU = mybir.AluOpType
AX = mybir.AxisListType

D = 1024
S = 2048
NF = D // 128
NT = S // 128
TW = S + 1
H = 16
DH = 64
W1F = 4096
WIN = 256
NWIN = S // WIN
DIN = 2730
DEPTH_IN = 6
NEG = -1.0e30


class Sem:
    __slots__ = ("h", "name", "count")

    def __init__(self, handle, name):
        self.h = handle
        self.name = name
        self.count = 0


class Stream:
    def __init__(self, prog, name):
        self.prog = prog
        self.name = name
        self.ops = []

    def op(self, method, *args, inc=None, **kwargs):
        incs = []
        if inc is not None:
            for it in (inc if isinstance(inc, list) else [inc]):
                sem, n = it if isinstance(it, tuple) else (it, 1)
                sem.count += n
                incs.append((sem, n))
        self.ops.append(("op", method, args, kwargs, incs))
        if incs:
            sem, _ = incs[-1]
            return (sem, sem.count)
        return None

    def dma(self, sem_name, *args, first=False, **kwargs):
        b = self.prog
        if first:
            self.wait(b.dsl.get(sem_name))
        tok = self.op("dma_start", *args, inc=(b.sm[sem_name], 16), **kwargs)
        b.dsl[sem_name] = tok
        return tok

    def opw(self, method, *args, sem=None, **kwargs):
        tok = self.op(method, *args, inc=sem, **kwargs)
        self.wait(tok)
        return tok

    def wait(self, tok):
        if tok is None:
            return
        if isinstance(tok, list):
            for t in tok:
                self.wait(t)
            return
        sem, val = tok if isinstance(tok, tuple) else (tok, tok.count)
        key = (sem.name, self.name)
        if self.prog.last_wait.get(key, -1) >= val:
            return
        self.prog.last_wait[key] = val
        self.ops.append(("wait", None, (sem, val), {}, []))

    def replay(self, eng):
        for kind, method, args, kwargs, incs in self.ops:
            if kind == "wait":
                sem, val = args
                eng.wait_ge(sem.h, val)
                continue
            ins = getattr(eng, method)(*args, **kwargs)
            for sem, n in incs:
                ins = ins.then_inc(sem.h, n)


class Build:

    def __init__(self, ctx, stages, dbg=()):
        self.ctx = ctx
        self.stages = stages
        self.dbg_names = set(dbg)
        self.dbg = {}
        self.nc = bass.Bass("TRN2", target_bir_lowering=False)
        self.last_wait = {}
        self.dsl = {}
        self.t = Stream(self, "t")
        self.v = Stream(self, "v")
        self.s = Stream(self, "s")
        self.g = Stream(self, "g")
        self.y = Stream(self, "y")
        self.inputs = []
        self.sm = {}
        for name in ("mm", "ve", "sc", "gp", "dy", "dg", "dg0", "dg1", "dy0", "dy1", "dyc", "dyx", "fin"):
            self.sm[name] = Sem(self.ctx.enter_context(self.nc.semaphore(name)), name)

    def inp(self, name, shape, dtype=F32):
        self.inputs.append(name)
        return self.nc.dram_tensor(name, list(shape), dtype, kind="ExternalInput").ap()

    def outp(self, name, shape, dtype=F32):
        return self.nc.dram_tensor(name, list(shape), dtype, kind="ExternalOutput").ap()

    def dram(self, name, shape, dtype=F32):
        return self.nc.dram_tensor(name, list(shape), dtype, kind="Internal").ap()

    def sb(self, name, shape, dtype=F32):
        return self.ctx.enter_context(self.nc.sbuf_tensor(name, list(shape), dtype))

    def ps(self, name, shape, dtype=F32):
        return self.ctx.enter_context(self.nc.psum_tensor(name, list(shape), dtype))

    def dbg_reg(self, name, sbuf_ap, tok):
        if name in self.dbg_names:
            o = self.outp("dbg_" + name, list(sbuf_ap.shape), sbuf_ap.dtype)
            self.dbg[name] = (o, sbuf_ap, tok)

    def flush_dbg(self):
        for name, (o, ap, tok) in self.dbg.items():
            self.y.wait(tok)
            self.y.dma("dy", out=o, in_=ap, first=True)

    def finalize(self):
        for nm in ("dy", "dg", "dg0", "dg1", "dy0", "dy1", "dyc", "dyx"):
            sm_ = self.sm[nm]
            if sm_.count:
                self.y.ops.append(("wait", None, (sm_, sm_.count), {}, []))
        blk = self.ctx.enter_context(self.nc.Block())
        blk.tensor(lambda e: self.t.replay(e))
        blk.vector(lambda e: self.v.replay(e))
        blk.scalar(lambda e: self.s.replay(e))
        blk.gpsimd(lambda e: self.g.replay(e))
        blk.sync(lambda e: self.y.replay(e))


def xt(bld, j, c0=0, n=S):
    return bld.xf[:, j * TW + 1 + c0: j * TW + 1 + c0 + n]


def xt_sh(bld, j, c0=0, n=S):
    return bld.xf[:, j * TW + c0: j * TW + c0 + n]


def host_constants():
    c = {}
    c["c_ident"] = np.eye(128, dtype=np.float32)
    c["c_ones_col_bf"] = np.ones((128, 1), dtype=np.float32)
    half = np.arange(0, DH, 2, dtype=np.float64)
    inv_freq = 1.0 / 10000.0 ** (half / DH)
    t = np.arange(S, dtype=np.float64)
    freqs = t[:, None] * inv_freq[None, :]
    freqs = np.concatenate([freqs, freqs], axis=1)
    base_scale = (half + 0.4 * DH) / (1.4 * DH)
    power = (t - S // 2) / (WIN // 2)
    scale = base_scale[None, :] ** power[:, None]
    scale = np.concatenate([scale, scale], axis=1)
    cos = np.cos(freqs)
    sin = np.sin(freqs)
    r8 = 8.0 ** 0.5
    qc = (cos * scale).T * r8
    qs = (sin * scale).T * r8
    kc = (cos / scale).T * r8
    ks = (sin / scale).T * r8
    c["c_rotq_cs"] = np.concatenate([qc, qc], 0).astype(np.float32)
    c["c_rotq_ss"] = np.concatenate([qs, qs], 0).astype(np.float32)
    c["c_rotk_cs"] = np.concatenate([kc, kc], 0).astype(np.float32)
    c["c_rotk_ss"] = np.concatenate([ks, ks], 0).astype(np.float32)
    J = np.zeros((128, 128), dtype=np.float32)
    for b in range(2):
        o = 64 * b
        for m in range(32):
            J[o + m + 32, o + m] = -1.0
        for m in range(32, 64):
            J[o + m - 32, o + m] = 1.0
    c["c_J"] = J
    selbc = np.zeros((8, 16, 128), dtype=np.float32)
    for tp in range(8):
        selbc[tp, 2 * tp, 0:64] = 1.0
        selbc[tp, 2 * tp + 1, 64:128] = 1.0
    c["c_selbc"] = selbc.reshape(8 * 16, 128)
    c["c_ones"] = np.ones((128, 128), dtype=np.float32)
    c["c_iota"] = np.arange(128, dtype=np.float32).reshape(128, 1)
    p = np.arange(128, dtype=np.float32)
    bands = np.stack([
        p, np.minimum(p + 257, 384),
        np.full(128, 256.0, dtype=np.float32), p + 257,
        np.full(128, 128.0, dtype=np.float32), np.minimum(p + 257, 384),
    ], axis=1).astype(np.float32)
    c["c_bandsA"] = bands
    mB = np.full((128, 6 * 256), NEG, dtype=np.float32)
    for ks in range(4):
        for pp in range(128):
            jk = ks * 128 + pp
            lo = max(0, jk - 256)
            hi = min(256, jk + 1)
            if hi > lo:
                mB[pp, ks * 256 + lo: ks * 256 + hi] = 0.0
    for bsub in range(2):
        for pp in range(128):
            lo = bsub * 128 + pp
            mB[pp, (4 + bsub) * 256 + lo: (4 + bsub) * 256 + 256] = 0.0
    c["c_maskB"] = mB
    c["c_l0diag"] = np.stack([p + 128 * d for d in range(4)], axis=1).astype(np.float32)
    c["c_eps_col"] = np.full((1, 1), 1e-5, dtype=np.float32)
    c["c_eps128"] = np.full((128, 1), 1e-5, dtype=np.float32)
    c["c_eps6"] = np.full((128, 1), 1e-6, dtype=np.float32)
    c["c_he8"] = np.full((1, 1), 0.5e-8, dtype=np.float32)
    c["c_eps24"] = np.full((128, 1), 1e-24, dtype=np.float32)
    sn8 = np.zeros((8, 4, 128), dtype=np.float32)
    for hh in range(4):
        sn8[hh, hh, :] = 1.0
        sn8[4 + hh, hh, :] = 1.0
    c["c_sn8"] = sn8.reshape(8, 512)
    sel8 = np.zeros((8, 4, 128), dtype=np.float32)
    for vv in range(4):
        part, tp = vv // 2, vv % 2
        sel8[part * 4 + tp * 2, vv, 0:64] = 1.0
        sel8[part * 4 + tp * 2 + 1, vv, 64:128] = 1.0
    c["c_sel8"] = sel8.reshape(8, 512)
    ones2 = np.zeros((128, 2), dtype=np.float32)
    ones2[0:64, 0] = 1.0
    ones2[64:128, 1] = 1.0
    c["c_ones2"] = ones2
    c["c_ln8"] = np.full((1, 1), -math.log(8.0), dtype=np.float32)
    dm = np.zeros((128, 512), dtype=np.float32)
    ii = np.arange(256)[None, :]
    pp = np.arange(128)[:, None]
    dm[:, 0:256] = np.where(ii >= pp, 0.0, NEG)
    dm[:, 256:512] = np.where(ii >= pp + 128, 0.0, NEG)
    c["c_l0dm"] = dm
    jj5 = np.arange(512)[None, :]
    jj2 = np.arange(256)[None, :]
    mA = np.full((128, 4 * 512), NEG, dtype=np.float32)
    for qs2 in range(2):
        lo = pp + qs2 * 128
        mA[:, qs2 * 512:(qs2 + 1) * 512] = np.where(
            (jj5 >= lo) & (jj5 < lo + 257), 0.0, NEG)
        mA[:, (2 + qs2) * 512:(2 + qs2) * 512 + 256] = np.where(
            jj2 <= pp + qs2 * 128, 0.0, NEG)
    c["c_maskA"] = mA
    return c


CONSTS = host_constants()


def emit_consts(bld):
    P = bld
    cb = {}
    specs = {
        "c_ident": ([128, 128], F32),
        "c_rotq_cs": ([128, S], BF16),
        "c_rotq_ss": ([128, S], BF16),
        "c_rotk_cs": ([128, S], BF16),
        "c_rotk_ss": ([128, S], BF16),
        "c_J": ([128, 128], BF16),
        "c_selbc": ([128, 128], BF16),
        "c_ones": ([128, 128], BF16),
        "c_iota": ([128, 1], F32),
        "c_bandsA": ([128, 6], F32),
        "c_maskB": ([128, 1536], BF16),
        "c_l0diag": ([128, 4], F32),
        "c_eps_col": ([1, 1], F32),
        "c_eps128": ([128, 1], F32),
        "c_eps6": ([128, 1], F32),
        "c_he8": ([1, 1], F32),
        "c_eps24": ([128, 1], F32),
        "c_sn8": ([8, 512], BF16),
        "c_sel8": ([8, 512], BF16),
        "c_ones2": ([128, 2], BF16),
        "c_ln8": ([1, 1], F32),
        "c_l0dm": ([128, 512], BF16),
        "c_maskA": ([128, 2048], BF16),
    }
    bld.cb_hbm = {}
    cfirst = True
    for name, (shape, dt) in specs.items():
        ap = bld.inp(name, shape, dt)
        if name.startswith("c_rot"):
            bld.cb_hbm[name] = ap
            continue
        t_ = bld.sb("s" + name, shape, dt)
        bld.y.dma("dyc", out=t_[:], in_=ap, first=cfirst)
        cfirst = False
        cb[name] = t_
    bld.cb = cb
    bld.tok_consts = (bld.sm["dyc"], bld.sm["dyc"].count)
    bld.ident = cb["c_ident"]
    bld.ones_bf = cb["c_ones"]


def const_np_inputs():
    import ml_dtypes
    out = {}
    out["c_ident"] = CONSTS["c_ident"]
    out["c_rotq_cs"] = CONSTS["c_rotq_cs"].astype(ml_dtypes.bfloat16)
    out["c_rotq_ss"] = CONSTS["c_rotq_ss"].astype(ml_dtypes.bfloat16)
    out["c_rotk_cs"] = CONSTS["c_rotk_cs"].astype(ml_dtypes.bfloat16)
    out["c_rotk_ss"] = CONSTS["c_rotk_ss"].astype(ml_dtypes.bfloat16)
    out["c_J"] = CONSTS["c_J"].astype(ml_dtypes.bfloat16)
    out["c_selbc"] = CONSTS["c_selbc"].astype(ml_dtypes.bfloat16)
    out["c_ones"] = CONSTS["c_ones"].astype(ml_dtypes.bfloat16)
    out["c_iota"] = CONSTS["c_iota"]
    out["c_bandsA"] = CONSTS["c_bandsA"]
    out["c_maskB"] = CONSTS["c_maskB"].astype(ml_dtypes.bfloat16)
    out["c_l0diag"] = CONSTS["c_l0diag"]
    out["c_eps_col"] = CONSTS["c_eps_col"]
    out["c_eps128"] = CONSTS["c_eps128"]
    out["c_eps6"] = CONSTS["c_eps6"]
    out["c_he8"] = CONSTS["c_he8"]
    out["c_eps24"] = CONSTS["c_eps24"]
    out["c_sn8"] = CONSTS["c_sn8"].astype(ml_dtypes.bfloat16)
    out["c_sel8"] = CONSTS["c_sel8"].astype(ml_dtypes.bfloat16)
    out["c_ones2"] = CONSTS["c_ones2"].astype(ml_dtypes.bfloat16)
    out["c_ln8"] = CONSTS["c_ln8"]
    out["c_l0dm"] = CONSTS["c_l0dm"].astype(ml_dtypes.bfloat16)
    out["c_maskA"] = CONSTS["c_maskA"].astype(ml_dtypes.bfloat16)
    return out


def emit_load_x(bld, src_ap):
    for j in range(NF):
        bld.g.op("memset", bld.xf[:, j * TW: j * TW + 1], 0.0,
                 inc=bld.sm["gp"])
        bld.y.dma("dyx", out=xt(bld, j),
                  in_=src_ap[j * 128:(j + 1) * 128, :], first=(j == 0))
    bld.tok_x = (bld.sm["dyx"], bld.sm["dyx"].count)
    bld.tok_guard = (bld.sm["gp"], bld.sm["gp"].count)


def emit_store_x(bld, dst_ap, tok):
    for j in range(NF):
        bld.y.wait(tok)
        bld.y.dma("dy", out=dst_ap[j * 128:(j + 1) * 128, :], in_=xt(bld, j),
                  first=(j == 0))


def stats_rows(bld, get_xb, get_xsq, psum_s, psum_q, n0, n1, deps):
    onescol = bld.ones_bf[:, 0:1]
    t = bld.t
    t.wait(deps)
    tok = None
    for j in range(NF):
        t.op("ldweights", onescol)
        tok = t.op("matmul", psum_s, onescol, get_xb(j),
                   start=(j == 0), stop=(j == NF - 1),
                   inc=bld.sm["mm"] if j == NF - 1 else None)
    for j in range(NF):
        t.op("ldweights", onescol)
        tok = t.op("matmul", psum_q, onescol, get_xsq(j),
                   start=(j == 0), stop=(j == NF - 1),
                   inc=bld.sm["mm"] if j == NF - 1 else None)
    return tok


AR_COLS = 25984


def alloc_bufs(bld):
    b = bld
    b.xf = b.sb("xf", [128, NF * TW], F32)
    b.act = b.sb("act", [128, NF * S], BF16)
    b.sc1 = b.sb("sc1", [128, 4096], BF16)
    b.sc2 = b.sb("sc2", [128, 4096], BF16)
    b.tl = b.sb("tl", [128, 512], F32)
    b.tl2 = b.sb("tl2", [128, 512], F32)
    b.lnrow = b.sb("lnrow", [1, 1536], F32)
    b.lnrowbf = b.sb("lnrowbf", [1, 1536], BF16)
    b.lrow = b.sb("lrow", [1, 512], F32)
    b.lrowbf = b.sb("lrowbf", [1, 1024], BF16)
    b.cols = b.sb("cols", [128, 256], F32)
    b.colsbf = b.sb("colsbf", [128, 64], BF16)
    b.rows32 = b.sb("rows32", [32, 512], F32)
    b.rows32bf = b.sb("rows32bf", [32, 512], BF16)
    b.pcol = b.sb("pcol", [128, 600], F32)
    b.mstage = b.sb("mstage", [128, 512], F32)
    b.prow = b.sb("prow", [1, 4096], BF16)
    b.ar = b.sb("ar", [128, AR_COLS], BF16)
    b.pA = b.ps("pA", [128, 512], F32)
    b.pB = b.ps("pB", [128, 512], F32)
    b.pC = b.ps("pC", [128, 512], F32)
    b.pD = b.ps("pD", [128, 512], F32)
    b.pW = b.ps("pW", [128, 1024], F32)
    b.pV = b.ps("pV", [128, 1024], F32)


def A(bld, off, p, n, dtype=BF16):
    assert off + n <= AR_COLS, f"arena overflow {off}+{n}"
    ap = bld.ar[0:p, off:off + n]
    if dtype == F32:
        ap = ap.bitcast(F32)
    return ap


def emit_ln_feature(bld, src_fn, dst_fn, dst_dtype, gcol=None, bcol=None,
                    src_is_bf=False, add_into_xf=False, src_tok=None,
                    reload_hbm=None, tag="ln"):
    b = bld
    t, v, s, g = b.t, b.v, b.s, b.g
    CH = 512
    mm, ve, sc, gp = b.sm["mm"], b.sm["ve"], b.sm["sc"], b.sm["gp"]
    onescol = b.ones_bf[:, 0:1]
    RLB = A(b, 0, 128, 8192, F32) if reload_hbm is not None else None
    rl_toks = [None] * NF
    prev_mm_toks = [None] * NF
    prev_chain_tok = None
    prev_apply_tok = None
    last_tok = None
    for c in range(S // CH):
        c0 = c * CH
        mm_toks = [None] * NF
        for j in range(NF):
            sl = j * CH
            s.wait(src_tok)
            g.wait(src_tok)
            v.wait(src_tok)
            if not src_is_bf:
                g.wait(prev_mm_toks[j])
                ctok = g.op("tensor_copy", out=b.sc1[:, sl:sl + CH],
                            in_=src_fn(j, c0, CH), inc=gp)
            else:
                ctok = None
            s.wait(prev_mm_toks[j])
            stok = s.op("activation", out=b.sc2[:, sl:sl + CH],
                        in_=src_fn(j, c0, CH), func=AF.Square, inc=sc)
            t.wait(ctok)
            t.wait(stok)
            if c == 0 and j == 0:
                pass
            t.wait(prev_chain_tok)
            xb_ap = (b.sc1[:, sl:sl + CH] if not src_is_bf
                     else src_fn(j, c0, CH))
            t.op("matmul", b.pA[0:1, :], onescol, xb_ap,
                 start=(j == 0), stop=(j == NF - 1))
            mm_toks[j] = t.op("matmul", b.pB[0:1, :], onescol,
                              b.sc2[:, sl:sl + CH],
                              start=(j == 0), stop=(j == NF - 1), inc=mm)
        prev_mm_toks = mm_toks
        stats_tok = mm_toks[NF - 1]
        R, RB = b.lnrow, b.lnrowbf
        rowA = R[0:1, 0:CH]
        rowB = R[0:1, 512:512 + CH]
        rowC = R[0:1, 1024:1024 + CH]
        mu_bf = RB[0:1, 0:CH]
        rhohi_bf = RB[0:1, 512:512 + CH]
        rholo_bf = RB[0:1, 1024:1024 + CH]
        s.wait(stats_tok)
        s.opw("activation", sem=sc, out=rowA, in_=b.pA[0:1, :], func=AF.Copy,
              scale=1.0 / D)
        prev_chain_tok = s.op("activation", out=rowB, in_=b.pB[0:1, :],
                              func=AF.Copy, scale=1.0 / D, inc=sc)
        tok_m = prev_chain_tok
        tok_sq = s.op("activation", out=rowC, in_=rowA, func=AF.Square, inc=sc)
        v.wait(tok_sq)
        tok_var = v.op("tensor_sub", out=rowB, in0=rowB, in1=rowC, inc=ve)
        s.wait(tok_var)
        s.opw("activation", sem=sc, out=rowB, in_=rowB, func=AF.Ln,
              bias=b.cb["c_eps_col"][:])
        tok_rho = s.op("activation", out=rowC, in_=rowB, func=AF.Exp,
                       scale=-0.5, inc=sc)
        v.wait(tok_rho)
        v.op("tensor_copy", out=mu_bf, in_=rowA)
        v.opw("tensor_copy", sem=ve, out=rhohi_bf, in_=rowC)
        v.opw("tensor_copy", sem=ve, out=rowB, in_=rhohi_bf)
        tok_rows = v.op("tensor_sub", out=rholo_bf, in0=rowC, in1=rowB,
                        inc=ve)
        t.wait(tok_rows)
        t.wait(prev_apply_tok)
        onesrow = b.ones_bf[0:1, :]
        t.op("matmul", b.pC[:, :], onesrow, mu_bf, start=True, stop=True)
        t.op("matmul", b.pD[:, :], onesrow, rhohi_bf, start=True, stop=False)
        tok_bc = t.op("matmul", b.pD[:, :], onesrow, rholo_bf, start=False,
                      stop=True, inc=mm)
        v.wait(tok_bc)
        if reload_hbm is not None:
            b.y.wait(prev_apply_tok)
            for j in range(NF):
                rl_toks[j] = b.y.dma(
                    "dy1", out=RLB[:, j * 512:(j + 1) * 512],
                    in_=reload_hbm[j * 128:(j + 1) * 128, c0:c0 + CH],
                    first=(j == 0))
        for j in range(NF):
            v.wait(last_tok)
            v.opw("tensor_sub", sem=ve, out=b.tl[:, 0:CH],
                  in0=src_fn(j, c0, CH), in1=b.pC[:, :])
            if reload_hbm is not None:
                v.wait(rl_toks[NF - 1])
                v.opw("tensor_mul", sem=ve, out=b.tl2[:, 0:CH],
                      in0=b.tl[:, 0:CH], in1=b.pD[:, :])
                last_tok = v.op("tensor_add", out=xt(b, j, c0, CH),
                                in0=RLB[:, j * 512:(j + 1) * 512],
                                in1=b.tl2[:, 0:CH], inc=ve)
                continue
            if add_into_xf:
                v.opw("tensor_mul", sem=ve, out=b.tl2[:, 0:CH],
                      in0=b.tl[:, 0:CH], in1=b.pD[:, :])
                last_tok = v.op("tensor_add", out=xt(b, j, c0, CH),
                                in0=xt(b, j, c0, CH), in1=b.tl2[:, 0:CH],
                                inc=ve)
            elif gcol is not None:
                v.opw("scalar_tensor_tensor", sem=ve, out=b.tl2[:, 0:CH],
                      in0=b.tl[:, 0:CH], scalar=gcol[j], in1=b.pD[:, :],
                      op0=ALU.mult, op1=ALU.mult)
                last_tok = v.op("tensor_scalar_add", out=dst_fn(j, c0, CH),
                                in0=b.tl2[:, 0:CH], scalar1=bcol[j], inc=ve)
            else:
                last_tok = v.op("tensor_mul", out=dst_fn(j, c0, CH),
                                in0=b.tl[:, 0:CH], in1=b.pD[:, :], inc=ve)
        prev_apply_tok = last_tok
    return last_tok


def emit_l0_attn_sandwich(bld, wq, wk, wv, wo):
    b = bld
    t, v, s, g, y = b.t, b.v, b.s, b.g, b.y
    mm, ve, sc, gp, dg, dy = (b.sm[k] for k in ("mm", "ve", "sc", "gp", "dg", "dy"))
    ATNs = lambda r0, nr, c0, n: b.ar[r0:r0 + nr, c0:c0 + n]
    KHG = A(b, 16384, 128, 2048)
    VHG = A(b, 18432, 128, 2080)
    WHG = A(b, 20512, 128, 1024)
    QCB = A(b, 21536, 128, 256)
    PSG = A(b, 21792, 128, 2048)
    WOB = A(b, 16384, 128, 4096)
    R = b.rows32
    kscr = R[0:1, 300:332]
    lnK2 = R[0:1, 332:348]
    tmp2a = R[0:1, 352:354]
    tmp2b = R[0:1, 356:358]
    tmp2c = R[0:1, 360:362]

    tok_xc = emit_ln_feature(
        b, src_fn=lambda j, c0, n: xt(b, j, c0, n),
        dst_fn=lambda j, c0, n: b.act[:, j * S + c0: j * S + c0 + n],
        dst_dtype=BF16, tag="l0ln1")
    act_tile = lambda j, c0, n: b.act[:, j * S + c0: j * S + c0 + n]

    ones2 = b.cb["c_ones2"]
    prev_z_tok = None
    prev_zlo = None
    prev_lnk2 = None
    prev_mtq = None
    for hg in range(8):
        v.wait(prev_lnk2)
        tok_wk = None
        for kk in range(NF):
            g.wait(prev_mtq)
            tok_wk = g.dma("dg", out=WHG[:, kk * 128:(kk + 1) * 128],
                           in_=wk[kk * 128:(kk + 1) * 128, hg * 128:(hg + 1) * 128],
                           first=(kk == 0))
        kt_mm_last = None
        e1 = e2 = None
        for th in range(2):
            t.wait(tok_wk)
            t.wait(tok_xc)
            t.wait(e1)
            t.wait(e2)
            for ch in range(2):
                for kk in range(NF):
                    kt_mm_last = t.op(
                        "matmul", b.pW[:, ch * 512:(ch + 1) * 512],
                        WHG[:, kk * 128:(kk + 1) * 128],
                        act_tile(kk, th * 1024 + ch * 512, 512),
                        start=(kk == 0), stop=(kk == NF - 1),
                        inc=mm if kk == NF - 1 else None)
            s.wait(kt_mm_last)
            e1 = s.op("activation", out=KHG[:, th * 1024:(th + 1) * 1024],
                      in_=b.pW[:, :], func=AF.Copy, inc=sc)
            e2 = s.op("activation", out=b.sc2[:, 0:1024], in_=b.pW[:, :],
                      func=AF.Square, inc=sc)
            t.wait(e2)
            prev_red = [None, None]
            for u in range(2):
                for ch in range(2):
                    t.wait(prev_red[ch])
                    mt2 = t.op("matmul", b.pV[0:1, ch * 512:(ch + 1) * 512],
                               ones2[:, u:u + 1],
                               b.sc2[:, ch * 512:(ch + 1) * 512],
                               start=True, stop=True, inc=mm)
                    v.wait(mt2)
                    dst = kscr[:, 2 * th + u: 2 * th + u + 1]
                    if ch == 0:
                        prev_red[ch] = v.op(
                            "reduce_max", out=dst,
                            in_=b.pV[0:1, ch * 512:(ch + 1) * 512],
                            axis=AX.X, inc=ve)
                    else:
                        v.opw("reduce_max", sem=ve, out=R[0:1, 364:365],
                              in_=b.pV[0:1, ch * 512:(ch + 1) * 512],
                              axis=AX.X)
                        v.wait(prev_red[0])
                        prev_red[ch] = v.op("tensor_max", out=dst, in0=dst,
                                            in1=R[0:1, 364:365], inc=ve)
            tok_kn = v.op("engine_nop", inc=ve)
            t.wait(tok_kn)
        v.wait(prev_red[0])
        v.wait(prev_red[1])
        v.op("tensor_max", out=kscr[:, 0:2], in0=kscr[:, 0:2], in1=kscr[:, 2:4])
        tok_k2 = v.op("engine_nop", inc=ve)
        s.wait(tok_k2)
        tok_lnk2 = s.op("activation", out=lnK2[:, 2 * hg:2 * hg + 2],
                        in_=kscr[:, 0:2], func=AF.Ln, inc=sc)
        prev_lnk2 = tok_lnk2
        tok_wv = None
        for kk in range(NF):
            g.wait(kt_mm_last)
            tok_wv = g.dma("dg", out=WHG[:, kk * 128:(kk + 1) * 128],
                           in_=wv[kk * 128:(kk + 1) * 128, hg * 128:(hg + 1) * 128],
                           first=(kk == 0))
        g.wait(prev_z_tok)
        vones_ap = VHG[:].rearrange("p (t c) -> p t c", c=65)[:, :, 64:65]
        tok_vones = g.op("memset", vones_ap, 1.0, inc=gp)
        v_evac = [None] * NT
        v_mm_last = None
        psl4 = [b.pA, b.pB, b.pC, b.pD]
        for tt in range(NT):
            t.wait(tok_wv)
            t.wait(tok_vones)
            bank = psl4[tt % 4]
            if tt >= 4:
                t.wait(v_evac[tt - 4])
            for kk in range(NF):
                v_mm_last = t.op(
                    "matmul", bank[:, 0:128],
                    act_tile(kk, tt * 128, 128),
                    WHG[:, kk * 128:(kk + 1) * 128],
                    start=(kk == 0), stop=(kk == NF - 1),
                    inc=mm if kk == NF - 1 else None)
            v.wait(v_mm_last)
            dst = VHG[:, tt * 130: tt * 130 + 130].rearrange(
                "p (u c) -> p u c", c=65)[:, :, 0:64]
            srcp = bank[:, 0:128].rearrange("p (u c) -> p u c", c=64)
            v_evac[tt] = v.op("tensor_copy", out=dst, in_=srcp, inc=ve)
        tok_wq = None
        for kk in range(NF):
            g.wait(v_mm_last)
            tok_wq = g.dma("dg", out=WHG[:, kk * 128:(kk + 1) * 128],
                           in_=wq[kk * 128:(kk + 1) * 128, hg * 128:(hg + 1) * 128],
                           first=(kk == 0))
        prev_eq = None
        prev_pv_evac = [None, None]
        for qc in range(8):
            t.wait(tok_wq)
            t.wait(v_evac[NT - 1])
            t.wait(prev_eq)
            for kk in range(NF):
                mtq = t.op("matmul", b.pB[:, 0:256],
                           WHG[:, kk * 128:(kk + 1) * 128],
                           act_tile(kk, qc * 256, 256),
                           start=(kk == 0), stop=(kk == NF - 1),
                           inc=mm if kk == NF - 1 else None)
            s.wait(mtq)
            eq = s.op("activation", out=QCB[:, :], in_=b.pB[:, 0:256],
                      func=AF.Copy, inc=sc)
            prev_eq = s.op("activation", out=b.sc2[:, 3072:3328],
                           in_=b.pB[:, 0:256], func=AF.Square, inc=sc)
            t.wait(prev_eq)
            for ts in range(2):
                mtn = t.op("matmul", b.pC[:, ts * 2: ts * 2 + 2],
                           b.sc2[:, 3072 + ts * 128: 3072 + (ts + 1) * 128],
                           ones2[:, 0:2], start=True, stop=True,
                           inc=mm if ts == 1 else None)
            v.wait(mtn)
            vq = v.op("tensor_copy", out=b.cols[:, 0:4], in_=b.pC[:, 0:4],
                      inc=ve)
            t.wait(vq)
            for ts in range(2):
                ttr = t.op("transpose", b.pD[0:2, ts * 128:(ts + 1) * 128],
                           b.cols[:, ts * 2: ts * 2 + 2], b.ident[:],
                           inc=mm if ts == 1 else None)
            v.wait(ttr)
            v.opw("tensor_copy", sem=ve, out=R[0:2, 0:256], in_=b.pD[0:2, 0:256])
            tok_qn = v.op("reduce_max", out=R[0:2, 256:257],
                          in_=R[0:2, 0:256], axis=AX.X, inc=ve)
            t.wait(tok_qn)
            ttr2 = t.op("transpose", b.pD[0:1, 256:258], R[0:2, 256:257],
                        b.ident[0:2, 0:2], inc=mm)
            s.wait(ttr2)
            s.wait(tok_lnk2)
            tok_lnq = s.op("activation", out=tmp2a, in_=b.pD[0:1, 256:258],
                           func=AF.Ln, inc=sc)
            v.wait(tok_lnq)
            tok_add = v.op("tensor_add", out=tmp2b, in0=tmp2a,
                           in1=lnK2[:, 2 * hg:2 * hg + 2], inc=ve)
            s.wait(tok_add)
            s.opw("activation", sem=sc, out=tmp2c, in_=tmp2b, func=AF.Exp,
                  scale=0.5, bias=b.cb["c_ln8"][:])
            tok_mh = s.op("activation", out=b.colsbf[0:1, 0:2],
                          in_=tmp2c, func=AF.Copy, scale=-1.0, inc=sc)
            t.wait(tok_mh)
            for u in range(2):
                tb = t.op("matmul", b.pC[:, 8 + u: 9 + u], b.ones_bf[0:1, :],
                          b.colsbf[0:1, u: u + 1], start=True,
                          stop=True, inc=mm if u == 1 else None)
            v.wait(tb)
            tok_negm = v.op("tensor_copy", out=b.cols[:, 8:10],
                            in_=b.pC[:, 8:10], inc=ve)
            t.wait(e1)
            t.wait(e2)
            t.wait(eq)
            nkt = 2 * qc + 2
            for u in range(2):
                t.wait(prev_pv_evac[u])
                pv_region = b.pV[0:65, u * 256:(u + 1) * 256]
                first_pv = True
                mts = None
                mpv = None
                for g0 in range(0, nkt, 4):
                    kts = list(range(g0, min(g0 + 4, nkt)))
                    for i, kt in enumerate(kts):
                        mts = t.op(
                            "matmul", b.pW[:, i * 256:(i + 1) * 256],
                            KHG[u * 64:(u + 1) * 64, kt * 128:(kt + 1) * 128],
                            QCB[u * 64:(u + 1) * 64, :],
                            start=True, stop=True,
                            inc=mm if i == len(kts) - 1 else None)
                    s.wait(mts)
                    v.wait(mts)
                    s.wait(tok_negm)
                    etok = None
                    for i, kt in enumerate(kts):
                        seg = b.pW[:, i * 256:(i + 1) * 256]
                        if kt >= 2 * qc:
                            dsel = kt - 2 * qc
                            tmr = v.op("tensor_add",
                                       out=b.mstage[:, dsel * 256:(dsel + 1) * 256],
                                       in0=seg,
                                       in1=b.cb["c_l0dm"][:, dsel * 256:(dsel + 1) * 256],
                                       inc=ve)
                            s.wait(tmr)
                            seg = b.mstage[:, dsel * 256:(dsel + 1) * 256]
                        etok = s.op("activation",
                                    out=PSG[:, u * 1024 + i * 256:
                                            u * 1024 + (i + 1) * 256],
                                    in_=seg, func=AF.Exp, scale=0.125,
                                    bias=b.cols[:, 8 + u: 9 + u], inc=sc)
                    t.wait(etok)
                    for i, kt in enumerate(kts):
                        mpv = t.op(
                            "matmul", pv_region,
                            VHG[:, kt * 130 + u * 65: kt * 130 + (u + 1) * 65],
                            PSG[:, u * 1024 + i * 256: u * 1024 + (i + 1) * 256],
                            start=first_pv, stop=(kt == nkt - 1),
                            inc=mm if kt == nkt - 1 else None)
                        first_pv = False
                v.wait(mpv)
                tok_atn = v.op(
                    "tensor_copy",
                    out=ATNs(u * 64, 64, hg * 2048 + qc * 256, 256),
                    in_=pv_region[0:64, :], inc=ve)
                s.wait(mpv)
                s.wait(prev_zlo)
                tok_l = s.op("activation",
                             out=b.lrow[0:1, u * 256:(u + 1) * 256],
                             in_=pv_region[64:65, :], func=AF.Copy, inc=sc)
                prev_pv_evac[u] = [tok_atn, tok_l]
            zf = b.lrow[0:1, 0:512]
            s.wait(tok_l)
            s.opw("activation", sem=sc, out=zf, in_=zf, func=AF.Ln)
            tok_e = s.op("activation", out=zf, in_=zf, func=AF.Exp,
                         scale=-1.0, inc=sc)
            v.wait(tok_e)
            v.opw("tensor_copy", sem=ve, out=b.lrowbf[0:1, 0:512], in_=zf)
            v.opw("tensor_copy", sem=ve, out=b.mstage[0:1, 0:512],
                  in_=b.lrowbf[0:1, 0:512])
            prev_zlo = v.op("tensor_sub", out=b.lrowbf[0:1, 512:1024],
                            in0=zf, in1=b.mstage[0:1, 0:512], inc=ve)
            t.wait(prev_zlo)
            for u in range(2):
                for hl in range(2):
                    zrow = b.lrowbf[0:1, hl * 512 + u * 256:
                                    hl * 512 + (u + 1) * 256]
                    tz = t.op("matmul", b.pC[u * 64:(u + 1) * 64, 16:272],
                              b.ones_bf[0:1, 0:64], zrow,
                              start=(hl == 0), stop=(hl == 1),
                              inc=mm if (u == 1 and hl == 1) else None)
            v.wait(tz)
            v.wait(prev_pv_evac[0])
            v.wait(prev_pv_evac[1])
            tok_zm = v.op("tensor_mul",
                          out=ATNs(0, 128, hg * 2048 + qc * 256, 256),
                          in0=ATNs(0, 128, hg * 2048 + qc * 256, 256),
                          in1=b.pC[:, 16:272], inc=ve)
            t.wait(tok_zm)
            prev_z_tok = tok_zm
            prev_mtq = mtq
    psl = [b.pA, b.pB, b.pC, b.pD]
    mo = None
    for half in range(2):
        tok_wo = None
        for kk in range(NF):
            g.wait(prev_z_tok)
            g.wait(mo)
            tok_wo = g.dma("dg", out=WOB[:, kk * 512:(kk + 1) * 512],
                           in_=wo[kk * 128:(kk + 1) * 128,
                                  half * 512:(half + 1) * 512],
                           first=(kk == 0))
        t.wait(tok_wo)
        t.wait(prev_z_tok)
        for qc4 in range(4):
            ev = None
            for oc4 in range(4):
                oc = half * 4 + oc4
                mo = None
                for kk in range(NF):
                    mo = t.op(
                        "matmul", psl[oc4][:, :],
                        WOB[:, kk * 512 + oc4 * 128: kk * 512 + (oc4 + 1) * 128],
                        b.ar[0:128, kk * 2048 + qc4 * 512: kk * 2048 + qc4 * 512 + 512],
                        start=(kk == 0), stop=(kk == NF - 1),
                        inc=mm if kk == NF - 1 else None)
                s.wait(mo)
                ev = s.op("activation",
                          out=b.act[:, oc * S + qc4 * 512: oc * S + qc4 * 512 + 512],
                          in_=psl[oc4][:, :], func=AF.Copy, inc=sc)
            t.wait(ev)
    tok_fin = emit_ln_feature(
        b, src_fn=lambda j, c0, n: b.act[:, j * S + c0: j * S + c0 + n],
        dst_fn=None, dst_dtype=None, src_is_bf=True, add_into_xf=True,
        src_tok=ev, tag="l0ln2")
    return tok_fin


def emit_l0_ffn_sandwich(bld, w1, w2, b1c, b2c, w1t_hbm, w2t_hbm, tok_params):
    b = bld
    t, v, s, g, y = b.t, b.v, b.s, b.g, b.y
    mm, ve, sc, gp, dg, dy = (b.sm[k] for k in ("mm", "ve", "sc", "gp", "dg", "dy"))
    FW = lambda sl: A(b, sl * 2048, 128, 2048, F32)
    TOUT = lambda sl: A(b, 8192 + sl * 2048, 128, 2048)
    W1B = lambda sl: A(b, sl * 4096, 128, 4096)
    W2B = A(b, 8192, 128, 4096)
    HFC = A(b, 12288, 128, 2048)
    OAC = A(b, 14336, 128, 8192, F32)
    ABSF = lambda sl: A(b, 14336 + sl * 2048, 128, 2048, F32)

    tok_xc = emit_ln_feature(
        b, src_fn=lambda j, c0, n: xt(b, j, c0, n),
        dst_fn=lambda j, c0, n: b.act[:, j * S + c0: j * S + c0 + n],
        dst_dtype=BF16, tag="ffln1")
    act_tile = lambda j, c0, n: b.act[:, j * S + c0: j * S + c0 + n]

    R = b.rows32
    thr_cols = [b.cols[:, 34:35], b.cols[:, 35:36]]
    tok_thrc = [None, None]
    prev_dma = [None, None]
    prev_red = None
    si = 0
    for wi, (wsrc, nrow, ncol) in enumerate(((w1, 1024, 4096),
                                             (w2, 4096, 1024))):
        stat_col = b.cols[:, 32 + wi: 33 + wi]
        tmp_col = b.cols[:, 38:39]
        first = True
        for rt in range(nrow // 128):
            for cc in range(ncol // 1024):
                sl = si % 2
                si += 1
                g.wait(prev_dma[sl])
                tok_d = g.dma("dg%d" % sl, out=FW(sl)[:, :],
                              in_=wsrc[rt * 128:(rt + 1) * 128,
                                       cc * 1024:(cc + 1) * 1024],
                              first=True)
                v.wait(tok_d)
                if first:
                    prev_red = v.op("reduce_sum", out=stat_col, in_=FW(sl)[:, :],
                                    axis=AX.X, apply_absolute_value=True, inc=ve)
                    first = False
                else:
                    v.wait(prev_red)
                    v.opw("reduce_sum", sem=ve, out=tmp_col, in_=FW(sl)[:, :],
                          axis=AX.X, apply_absolute_value=True)
                    prev_red = v.op("tensor_add", out=stat_col, in0=stat_col,
                                    in1=tmp_col, inc=ve)
                prev_dma[sl] = prev_red
        v.wait(prev_red)
        v.opw("tensor_copy", sem=ve, out=b.colsbf[:, 8 + wi: 9 + wi],
              in_=stat_col)
        v.opw("tensor_copy", sem=ve, out=tmp_col,
              in_=b.colsbf[:, 8 + wi: 9 + wi])
        tok_cbf = v.op("tensor_sub", out=b.colsbf[:, 12 + wi: 13 + wi],
                       in0=stat_col, in1=tmp_col, inc=ve)
        t.wait(tok_cbf)
        t.op("matmul", b.pA[0:1, wi:wi + 1], b.ones_bf[:, 0:1],
             b.colsbf[:, 8 + wi: 9 + wi], start=True, stop=False)
        tmm = t.op("matmul", b.pA[0:1, wi:wi + 1], b.ones_bf[:, 0:1],
                   b.colsbf[:, 12 + wi: 13 + wi], start=False, stop=True, inc=mm)
        s.wait(tmm)
        tok_sc = s.op("activation", out=R[0:1, 400 + wi * 4: 401 + wi * 4],
                      in_=b.pA[0:1, wi:wi + 1], func=AF.Copy,
                      scale=0.5 / (4096.0 * 1024.0), inc=sc)
        v.wait(tok_sc)
        v.opw("tensor_max", sem=ve, out=R[0:1, 400 + wi * 4: 401 + wi * 4],
              in0=R[0:1, 400 + wi * 4: 401 + wi * 4], in1=b.cb["c_he8"][:])
        v.opw("tensor_copy", sem=ve, out=b.colsbf[0:1, 16 + wi * 2: 17 + wi * 2],
              in_=R[0:1, 400 + wi * 4: 401 + wi * 4])
        v.opw("tensor_copy", sem=ve, out=R[0:1, 401 + wi * 4: 402 + wi * 4],
              in_=b.colsbf[0:1, 16 + wi * 2: 17 + wi * 2])
        tok_thr = v.op("tensor_sub", out=b.colsbf[0:1, 17 + wi * 2: 18 + wi * 2],
                       in0=R[0:1, 400 + wi * 4: 401 + wi * 4],
                       in1=R[0:1, 401 + wi * 4: 402 + wi * 4], inc=ve)
        t.wait(tok_thr)
        t.op("matmul", b.pB[:, wi:wi + 1], b.ones_bf[0:1, :],
             b.colsbf[0:1, 16 + wi * 2: 17 + wi * 2], start=True, stop=False)
        tbc = t.op("matmul", b.pB[:, wi:wi + 1], b.ones_bf[0:1, :],
                   b.colsbf[0:1, 17 + wi * 2: 18 + wi * 2], start=False,
                   stop=True, inc=mm)
        v.wait(tbc)
        tok_thrc[wi] = v.op("tensor_copy", out=thr_cols[wi],
                            in_=b.pB[:, wi:wi + 1], inc=ve)
    prev_t = [None, None]
    prev_sa = [prev_red, prev_red]
    prev_absr = [None, None]
    tok_spill = None
    si = 0
    for wi, (wsrc, wdst, nrow, ncol) in enumerate(((w1, w1t_hbm, 1024, 4096),
                                                   (w2, w2t_hbm, 4096, 1024))):
        thr_col = thr_cols[wi]
        for rt in range(nrow // 128):
            for cc in range(ncol // 1024):
                sl = si % 2
                si += 1
                g.wait(prev_sa[sl])
                tok_d = g.dma("dg%d" % sl, out=FW(sl)[:, :],
                              in_=wsrc[rt * 128:(rt + 1) * 128,
                                       cc * 1024:(cc + 1) * 1024],
                              first=True)
                s.wait(tok_d)
                s.wait(prev_t[sl])
                s.wait(prev_absr[sl])
                tok_a = s.op("activation", out=ABSF(sl)[:, :],
                             in_=FW(sl)[:, :], func=AF.Abs, inc=sc)
                tok_sg = s.op("activation", out=b.sc2[:, sl * 1024:(sl + 1) * 1024],
                              in_=FW(sl)[:, :], func=AF.Sign, inc=sc)
                prev_sa[sl] = tok_sg
                v.wait(tok_sg)
                v.wait(tok_thrc[wi])
                v.wait(prev_t[sl])
                v.wait(tok_a)
                v.opw("tensor_single_scalar", sem=ve, out=b.tl[:, 0:512],
                      in_=ABSF(sl)[:, 0:512],
                      scalar=thr_col, op=ALU.is_ge)
                prev_absr[sl] = v.opw("tensor_single_scalar", sem=ve,
                                      out=b.tl2[:, 0:512],
                                      in_=ABSF(sl)[:, 512:1024],
                                      scalar=thr_col, op=ALU.is_ge)
                v.opw("tensor_mul", sem=ve, out=TOUT(sl)[:, 0:512],
                      in0=b.tl[:, 0:512],
                      in1=b.sc2[:, sl * 1024: sl * 1024 + 512])
                tok_w = v.op("tensor_mul", out=TOUT(sl)[:, 512:1024],
                             in0=b.tl2[:, 0:512],
                             in1=b.sc2[:, sl * 1024 + 512:(sl + 1) * 1024],
                             inc=ve)
                y.wait(tok_w)
                tok_spill = y.dma("dy%d" % sl,
                                  out=wdst[rt * 128:(rt + 1) * 128,
                                           cc * 1024:(cc + 1) * 1024],
                                  in_=TOUT(sl)[:, 0:1024],
                                  first=True)
                prev_t[sl] = tok_spill
    prev_w1use = [tok_spill, tok_spill]
    prev_w2use = tok_spill
    prev_gelu = [None] * 4
    prev_oadd = [None] * NF
    prev_hmm = {}
    bo_toks = []
    for tokc in range(4):
        for fc in range(8):
            sl = fc % 2
            g.wait(prev_w1use[sl])
            g.wait(prev_t[0])
            g.wait(prev_t[1])
            tok_w1 = None
            for kk in range(NF):
                tok_w1 = g.dma("dg%d" % sl,
                               out=W1B(sl)[:, kk * 512:(kk + 1) * 512],
                               in_=w1t_hbm[kk * 128:(kk + 1) * 128,
                                           fc * 512:(fc + 1) * 512],
                               first=(kk == 0))
            y.wait(prev_w2use)
            y.wait(prev_t[0])
            y.wait(prev_t[1])
            tok_w2 = None
            for fsub in range(4):
                tok_w2 = y.dma("dy%d" % (fc % 2),
                               out=W2B[:, fsub * 1024:(fsub + 1) * 1024],
                               in_=w2t_hbm[fc * 512 + fsub * 128:
                                           fc * 512 + (fsub + 1) * 128, :],
                               first=(fsub == 0))
            for fsub in range(4):
                bank = [b.pA, b.pB][fsub % 2]
                t.wait(tok_w1)
                t.wait(tok_xc)
                t.wait(prev_gelu[fsub % 2])
                hm = None
                for kk in range(NF):
                    hm = t.op("matmul", bank[:, :],
                              W1B(sl)[:, kk * 512 + fsub * 128:
                                      kk * 512 + (fsub + 1) * 128],
                              act_tile(kk, tokc * 512, 512),
                              start=(kk == 0), stop=(kk == NF - 1),
                              inc=mm if kk == NF - 1 else None)
                prev_w1use[sl] = hm
                s.wait(hm)
                prev_gelu[fsub % 2] = s.op(
                    "activation", out=HFC[:, fsub * 512:(fsub + 1) * 512],
                    in_=bank[:, :], func=AF.Gelu,
                    bias=b1c[:, fc * 4 + fsub: fc * 4 + fsub + 1], inc=sc)
            for dc in range(NF):
                bank = [b.pC, b.pD][dc % 2]
                t.wait(tok_w2)
                t.wait(prev_gelu[0])
                t.wait(prev_gelu[1])
                t.wait(prev_oadd[dc])
                t.wait(prev_oadd[(dc - 2) % NF])
                om = None
                for fsub in range(4):
                    om = t.op("matmul", bank[:, :],
                              W2B[:, fsub * 1024 + dc * 128:
                                  fsub * 1024 + (dc + 1) * 128],
                              HFC[:, fsub * 512:(fsub + 1) * 512],
                              start=(fsub == 0), stop=(fsub == 3),
                              inc=mm if fsub == 3 else None)
                prev_w2use = om
                v.wait(om)
                dst = OAC[:, dc * 512:(dc + 1) * 512]
                if fc == 0:
                    prev_oadd[dc] = v.op("tensor_scalar_add", out=dst,
                                         in0=bank[:, :],
                                         scalar1=b2c[:, dc:dc + 1], inc=ve)
                else:
                    prev_oadd[dc] = v.op("tensor_add", out=dst, in0=dst,
                                         in1=bank[:, :], inc=ve)
        for dc in range(NF):
            s.wait(prev_oadd[dc])
            bo_toks.append(s.op("activation",
                                out=b.act[:, dc * S + tokc * 512:
                                          dc * S + (tokc + 1) * 512],
                                in_=OAC[:, dc * 512:(dc + 1) * 512],
                                func=AF.Copy, inc=sc))
            v.wait(bo_toks[-1])
    tok_fin = emit_ln_feature(
        b, src_fn=lambda j, c0, n: b.act[:, j * S + c0: j * S + c0 + n],
        dst_fn=None, dst_dtype=None, src_is_bf=True, add_into_xf=True,
        src_tok=bo_toks[-1], tag="ffln2")
    return tok_fin


def emit_ln_token_add(bld, src_fn, src_toks, tts=None, prev_state=None,
                      tag="lnt"):
    b = bld
    t, v, s, g = b.t, b.v, b.s, b.g
    mm, ve, sc, gp = b.sm["mm"], b.sm["ve"], b.sm["sc"], b.sm["gp"]
    YF = b.sc1[:, 0:2048].bitcast(F32)
    prev_tr, prev_add = prev_state if prev_state else (None, None)
    if tts is None:
        tts = range(NT)
    for tt in tts:
        src = src_fn(tt)
        s.wait(src_toks)
        sq = s.op("activation", out=b.sc2[:, 0:1024], in_=src, func=AF.Square,
                  accum_out=b.cols[:, 40:41], inc=sc)
        v.wait(src_toks)
        sm_ = v.op("reduce_sum", out=b.cols[:, 41:42], in_=src, axis=AX.X,
                   inc=ve)
        s.wait(sm_)
        s.opw("activation", sem=sc, out=b.cols[:, 42:43], in_=b.cols[:, 41:42],
              func=AF.Copy, scale=1.0 / D)
        s.opw("activation", sem=sc, out=b.cols[:, 43:44], in_=b.cols[:, 42:43],
              func=AF.Square)
        s.opw("activation", sem=sc, out=b.cols[:, 44:45], in_=b.cols[:, 40:41],
              func=AF.Copy, scale=1.0 / D)
        v.wait((sc, b.sm["sc"].count))
        v.opw("tensor_sub", sem=ve, out=b.cols[:, 44:45], in0=b.cols[:, 44:45],
              in1=b.cols[:, 43:44])
        s.wait((ve, b.sm["ve"].count))
        s.opw("activation", sem=sc, out=b.cols[:, 44:45], in_=b.cols[:, 44:45],
              func=AF.Ln, bias=b.cb["c_eps128"][:])
        tok_rho = s.op("activation", out=b.cols[:, 45:46], in_=b.cols[:, 44:45],
                       func=AF.Exp, scale=-0.5, inc=sc)
        v.wait(tok_rho)
        v.wait(prev_tr)
        tok_y = v.op("tensor_scalar", out=YF[:, :], in0=src,
                     scalar1=b.cols[:, 42:43], scalar2=b.cols[:, 45:46],
                     op0=ALU.subtract, op1=ALU.mult, inc=ve)
        t.wait(tok_y)
        for j2 in range(0, NF, 4):
            trs = []
            for jj in range(4):
                j = j2 + jj
                bank = [b.pA, b.pB, b.pC, b.pD][jj]
                t.wait(prev_add)
                trs.append(t.op("transpose", bank[:, 0:128],
                                YF[:, j * 128:(j + 1) * 128], b.ident[:],
                                inc=mm))
            for jj in range(4):
                j = j2 + jj
                bank = [b.pA, b.pB, b.pC, b.pD][jj]
                v.wait(trs[jj])
                prev_add = v.op("tensor_add",
                                out=xt(b, j, tt * 128, 128),
                                in0=xt(b, j, tt * 128, 128),
                                in1=bank[:, 0:128], inc=ve)
        prev_tr = prev_add
    return prev_tr, prev_add


def emit_l1_attn_sandwich(bld, wqkv, bq_col, bkv_rows_tok):
    b = bld
    t, v, s, g, y = b.t, b.v, b.s, b.g, b.y
    mm, ve, sc, gp, dg, dy = (b.sm[k] for k in ("mm", "ve", "sc", "gp", "dg", "dy"))
    bkv_hi, bkv_lo, tok_bkv = bkv_rows_tok
    WKV = A(b, 0, 128, 16384)
    PKB = lambda sl: A(b, 16384 + sl * 1024, 128, 1024)
    PVB = lambda sl: A(b, 18432 + sl * 1040, 128, 1040)
    WQ = A(b, 0, 128, 8192)
    PQB = A(b, 8192, 128, 4096)
    ATB = A(b, 12288, 128, 4096)
    KVS = A(b, 16520, 128, 1040)
    PQ2 = A(b, 17560, 64, 8192)
    tok_xc = emit_ln_feature(
        b, src_fn=lambda j, c0, n: xt(b, j, c0, n),
        dst_fn=lambda j, c0, n: b.act[:, j * S + c0: j * S + c0 + n],
        dst_dtype=BF16, tag="l1aln1")
    act_tile = lambda j, c0, n: b.act[:, j * S + c0: j * S + c0 + n]

    tok_wkv = None
    for kk in range(0 if globals().get("NOWKV") else NF):
        tok_wkv = g.dma("dg", out=WKV[:, kk * 2048:(kk + 1) * 2048],
                        in_=wqkv[kk * 128:(kk + 1) * 128, 1024:3072],
                        first=(kk == 0))
    prev_phi = [None, None]
    kv_mm_tok = None
    for tt in range(0 if globals().get("NOP1") else NT):
        sl = tt % 2
        t.wait(tok_wkv)
        t.wait(tok_xc)
        t.wait(prev_phi[0])
        t.wait(prev_phi[1])
        mk = []
        for half, bank in ((0, b.pA), (1, b.pB), (2, b.pC), (3, b.pD)):
            for kk in range(NF):
                mt = t.op("matmul", bank[:, :],
                          act_tile(kk, tt * 128, 128),
                          WKV[:, kk * 2048 + half * 512: kk * 2048 + (half + 1) * 512],
                          start=(kk == 0), stop=False,
                          inc=None)
            t.wait(tok_bkv)
            t.op("matmul", bank[:, :], b.ones_bf[0:1, :],
                 bkv_hi[0:1, half * 512:(half + 1) * 512], start=False,
                 stop=False)
            mt = t.op("matmul", bank[:, :], b.ones_bf[0:1, :],
                      bkv_lo[0:1, half * 512:(half + 1) * 512], start=False,
                      stop=True, inc=mm)
            mk.append(mt)
        for half, bank in ((0, b.pA), (1, b.pB)):
            v.wait(mk[half])
            v.opw("tensor_scalar", sem=ve, out=b.tl[:, 0:512], in0=bank[:, :],
                  scalar1=0.0, scalar2=None, op0=ALU.min)
            s.wait((ve, b.sm["ve"].count))
            tke = s.op("activation", out=b.mstage[:, 0:512], in_=b.tl[:, 0:512],
                       func=AF.Exp, inc=sc)
            tkr = s.op("activation", out=b.tl2[:, 0:512], in_=bank[:, :],
                       func=AF.Relu, inc=sc)
            v.wait(tkr)
            prev_phi[sl] = v.op("tensor_add",
                                out=PKB(sl)[:, half * 512:(half + 1) * 512],
                                in0=b.mstage[:, 0:512], in1=b.tl2[:, 0:512],
                                inc=ve)
        g.wait(prev_phi[sl])
        tok_vo = g.op("memset",
                      PVB(sl)[:].rearrange("p (h c) -> p h c", c=65)[:, :, 64:65],
                      1.0, inc=gp)
        for half, bank in ((2, b.pC), (3, b.pD)):
            v.wait(mk[half])
            v.wait(tok_vo)
            dsth = PVB(sl)[:].rearrange("p (h c) -> p h c", c=65)[
                :, (half - 2) * 8:(half - 1) * 8, 0:64]
            prev_phi[sl] = v.op("tensor_copy", out=dsth,
                                in_=bank[:, :].rearrange("p (h c) -> p h c", c=64),
                                inc=ve)
        t.wait(prev_phi[sl])
        for h in range([] if globals().get("NOKV") else range(H), H)[1] if False else range(0 if globals().get("NOKV") else H):
            bk = h // 7
            col = (h % 7) * 72
            kvp = (b.pV[0:64, bk * 512 + col: bk * 512 + col + 65] if bk < 2
                   else b.pW[0:64, col:col + 65])
            kv_mm_tok = t.op("matmul", kvp,
                             PKB(sl)[:, h * 64:(h + 1) * 64],
                             PVB(sl)[:, h * 65:(h + 1) * 65],
                             start=(tt == 0 and h in (0, 7, 14)),
                             stop=(tt == NT - 1 and h in (6, 13, 15)),
                             skip_group_check=True,
                             inc=mm if (tt == NT - 1 and h == H - 1) else None)
    v.wait(kv_mm_tok)
    for r0 in (() if globals().get("NOP1") else (0,)):
        v.opw("tensor_copy", sem=ve,
              out=KVS[r0:r0 + 64, 0:455].rearrange("p (h c) -> p h c", c=65),
              in_=b.pV[0:64, 0:504].rearrange("p (h c) -> p h c", c=72)[:, :, 0:65])
        v.opw("tensor_copy", sem=ve,
              out=KVS[r0:r0 + 64, 455:910].rearrange("p (h c) -> p h c", c=65),
              in_=b.pV[0:64, 512:1016].rearrange("p (h c) -> p h c", c=72)[:, :, 0:65])
        tok_kvs = v.op("tensor_copy",
                       out=KVS[r0:r0 + 64, 910:1040].rearrange("p (h c) -> p h c", c=65),
                       in_=b.pW[0:64, 0:144].rearrange("p (h c) -> p h c", c=72)[:, :, 0:65],
                       inc=ve)
    if globals().get("NOP1"):
        tok_kvs = tok_xc
    if globals().get("TRUNC"):
        b.dbg_names.add("kvs")
        b.dbg_reg("kvs", KVS[:, :], tok_kvs)
        b.flush_dbg()
        return tok_kvs
    tok_wq = None
    if not globals().get("NOWQ"):
        for kk in range(NF):
            g.wait(kv_mm_tok)
            tok_wq = g.dma("dg", out=WQ[:, kk * 1024:(kk + 1) * 1024],
                           in_=wqkv[kk * 128:(kk + 1) * 128, 0:1024],
                           first=(kk == 0))
    if globals().get("MEMWQ"):
        tok_mw = g.op("memset", WQ[:, :], 0.5, inc=gp)
        t.wait(tok_mw)
    prev_q = [None, None]
    prev_attn = [None, None]
    ln_state = None
    for tc in range(4):
        for oc in range(1 if globals().get("QOC1") else NF):
            bank = [b.pA, b.pB][oc % 2]
            t.wait(tok_wq)
            t.wait(prev_q[oc % 2])
            t.wait(prev_attn[0])
            mq = None
            nkk = 1 if globals().get("QMM1") else NF
            for kk in range(nkk):
                mq = t.op("matmul", bank[:, :],
                          WQ[:, kk * 1024 + oc * 128: kk * 1024 + (oc + 1) * 128],
                          act_tile(kk, tc * 512, 512),
                          start=(kk == 0), stop=(kk == nkk - 1),
                          inc=mm if kk == nkk - 1 else None)
            if globals().get("NOPHI"):
                v.wait(mq)
                prev_q[oc % 2] = v.op("tensor_copy",
                                      out=PQB[:, oc * 512:(oc + 1) * 512],
                                      in_=bank[:, :], inc=ve)
                continue
            v.wait(mq)
            v.opw("tensor_scalar", sem=ve, out=b.tl[:, 0:512], in0=bank[:, :],
                  scalar1=bq_col[:, oc:oc + 1], scalar2=0.0,
                  op0=ALU.add, op1=ALU.min)
            s.wait((ve, b.sm["ve"].count))
            tqe = s.op("activation", out=b.mstage[:, 0:512], in_=b.tl[:, 0:512],
                       func=AF.Exp, inc=sc)
            tqr = s.op("activation", out=b.tl2[:, 0:512], in_=bank[:, :],
                       func=AF.Relu, bias=bq_col[:, oc:oc + 1], inc=sc)
            v.wait(tqr)
            v.op("tensor_add",
                 out=PQ2[0:64, (2 * oc) * 512:(2 * oc + 1) * 512],
                 in0=b.mstage[0:64, 0:512], in1=b.tl2[0:64, 0:512])
            prev_q[oc % 2] = v.op("tensor_add",
                                  out=PQ2[0:64, (2 * oc + 1) * 512:
                                          (2 * oc + 2) * 512],
                                  in0=b.mstage[64:128, 0:512],
                                  in1=b.tl2[64:128, 0:512], inc=ve)
        for ts in range(4):
            t.wait(prev_q[0])
            t.wait(prev_q[1])
            t.wait(tok_kvs)
            t.wait(prev_attn[0])
            t.wait(prev_attn[1])
            ma = None
            for h in range(H):
                bk = h // 7
                col = (h % 7) * 72
                dst = (b.pW[:, bk * 512 + col: bk * 512 + col + 65] if bk < 2
                       else b.pV[:, col:col + 65])
                ma = t.op("matmul", dst,
                          PQ2[0:64, h * 512 + ts * 128:
                              h * 512 + (ts + 1) * 128],
                          KVS[0:64, h * 65:(h + 1) * 65],
                          start=True, stop=True,
                          inc=mm if h == H - 1 else None)
            v.wait(ma)
            tt_abs = tc * 4 + ts
            groups = [(0, b.pW, 0, 7), (1, b.pW, 512, 7), (2, b.pV, 0, 2)]
            for gi, pb, c0, nh in groups:
                h0 = gi * 7
                zsrc = pb[:, c0:c0 + nh * 72].rearrange(
                    "p (h c) -> p h c", c=72)[:, :, 64:65]
                v.opw("tensor_copy", sem=ve,
                      out=b.cols[:, 48 + h0: 48 + h0 + nh], in_=zsrc)
                v.opw("tensor_single_scalar", sem=ve,
                      out=b.cols[:, 48 + h0: 48 + h0 + nh],
                      in_=b.cols[:, 48 + h0: 48 + h0 + nh],
                      scalar=b.cb["c_eps6"][:], op=ALU.add)
                v.opw("reciprocal", sem=ve,
                      out=b.cols[:, 48 + h0: 48 + h0 + nh],
                      in_=b.cols[:, 48 + h0: 48 + h0 + nh])
                zb = b.cols[:, 48 + h0: 48 + h0 + nh].rearrange(
                    "p (h c) -> p h c", c=1)
                v.opw("tensor_copy", sem=ve, out=b.tl[:, 0:nh * 64]
                      .rearrange("p (h c) -> p h c", c=64),
                      in_=pb[:, c0:c0 + nh * 72].rearrange(
                          "p (h c) -> p h c", c=72)[:, :, 0:64])
                prev_attn[gi % 2] = v.op(
                    "tensor_tensor",
                    out=ATB[:, ts * 1024 + h0 * 64:
                            ts * 1024 + (h0 + nh) * 64].rearrange(
                        "p (h c) -> p h c", c=64),
                    in0=b.tl[:, 0:nh * 64].rearrange("p (h c) -> p h c", c=64),
                    in1=zb.broadcast_to([128, nh, 64]),
                    op=ALU.mult, inc=ve)
        if globals().get("TRUNC2") and tc == 0:
            b.dbg_names.add("atb")
            b.dbg_reg("atb", ATB[:, :], prev_attn[0])
            b.flush_dbg()
            return prev_attn[1]
        ln_state = emit_ln_token_add(
            b, src_fn=lambda tt: ATB[:, (tt - tc * 4) * 1024:
                                     (tt - tc * 4 + 1) * 1024],
            src_toks=[prev_attn[0], prev_attn[1]],
            tts=range(tc * 4, tc * 4 + 4), prev_state=ln_state, tag="l1ln2")
        prev_attn = [ln_state[1], ln_state[1]]
    return ln_state[1]


def emit_bkv_prep(bld, bqkv_hbm):
    b = bld
    v, y = b.v, b.y
    BST = A(b, 14336, 1, 8192, F32)
    tok_d = y.dma("dyc", out=BST[0:1, 0:2048], in_=bqkv_hbm[0:1, 1024:3072],
                  first=True)
    hi = b.prow[0:1, 0:2048]
    lo = b.prow[0:1, 2048:4096]
    v.wait(tok_d)
    v.opw("tensor_copy", sem=b.sm["ve"], out=hi, in_=BST[0:1, 0:2048])
    v.opw("tensor_copy", sem=b.sm["ve"], out=BST[0:1, 2048:4096], in_=hi)
    tok = v.op("tensor_sub", out=lo, in0=BST[0:1, 0:2048],
               in1=BST[0:1, 2048:4096], inc=b.sm["ve"])
    return hi, lo, tok




def emit_inner_attn(bld, lay, wqkv, wo, qs_col, ks_col):
    b = bld
    t, v, s, g, y = b.t, b.v, b.s, b.g, b.y
    mm, ve, sc, gp = b.sm["mm"], b.sm["ve"], b.sm["sc"], b.sm["gp"]
    WQK = A(b, 0, 128, 6144)
    WOR = A(b, 6144, 128, 2048)
    KRG = A(b, 8192, 64, 2048)
    VRG = A(b, 10240, 128, 1040)
    QW = A(b, 11280, 64, 1024)
    PSB = A(b, 12320, 128, 1024)
    AOW = A(b, 13344, 128, 512)
    MSB = A(b, 13856, 128, 2048, F32)
    QRW = A(b, 15904, 128, 1024)
    IVR = A(b, 16928, 8, 256)
    MRW = A(b, 17184, 8, 256)
    SC2 = A(b, 17440, 128, 2048, F32)
    MCH = A(b, 19488, 128, 512, F32)
    prev = b.inner_prev
    import os as _os
    STAGE = int(_os.environ.get("INNER_STAGE", "9"))

    frz = None
    for kk in range(NF):
        sh = 1 if kk < NF // 2 else 0
        g.wait(prev.get("act_free"))
        frz = g.op("tensor_copy", out=b.act[:, kk * S:(kk + 1) * S],
                   in_=b.xf[:, kk * TW + sh: kk * TW + sh + S], inc=gp)
    prev["frz"] = frz
    act_w = lambda kk, w: b.act[:, kk * S + w * 256: kk * S + (w + 1) * 256]

    for hg in range(4):
        h0 = hg * 4
        tok_w = None
        g.wait(prev.get("wqk_free"))
        for kk in range(NF):
            for part in range(3):
                tok_w = g.dma(
                    "dg0",
                    out=WQK[:, kk * 768 + part * 256: kk * 768 + (part + 1) * 256],
                    in_=wqkv[kk * 128:(kk + 1) * 128,
                             part * 1024 + h0 * 64: part * 1024 + (h0 + 4) * 64],
                    first=(kk == 0 and part == 0))
        tok_wo = None
        g.wait(prev.get("wor_free"))
        for r2 in range(2):
            tok_wo = g.dma("dg1", out=WOR[:, r2 * 1024:(r2 + 1) * 1024],
                           in_=wo[h0 * 64 + r2 * 128: h0 * 64 + (r2 + 1) * 128, :],
                           first=(r2 == 0))
        for w in range(NWIN):
            ring = w % 2
            tok_rot = None
            for ti, nm in enumerate(("c_rotq_cs", "c_rotq_ss",
                                     "c_rotk_cs", "c_rotk_ss")):
                y.wait(prev.get("rot_free"))
                tok_rot = y.dma("dyx", out=b.sc2[:, ti * 256:(ti + 1) * 256],
                                in_=b.cb_hbm[nm][:, w * 256:(w + 1) * 256],
                                first=(ti == 0))
            for part in ((0, 1) if not _os.environ.get("SKIP_QK") else ()):
                for tp in range(2):
                    bank = [b.pA, b.pB, b.pC, b.pD][part * 2 + tp]
                    t.wait(tok_w)
                    t.wait(frz)
                    t.wait(prev.get(("qke", part, tp)))
                    if part == 0:
                        t.wait(prev.get(("vev", tp)))
                    else:
                        t.wait(prev.get("bc_free") if tp == 0
                               else prev.get("j_free"))
                        t.wait(prev.get(("wo_acc", NF - 2 + tp)))
                    mq = None
                    for kk in range(NF):
                        mq = t.op(
                            "matmul", bank[:, 0:256],
                            WQK[:, kk * 768 + part * 256 + tp * 128:
                                kk * 768 + part * 256 + (tp + 1) * 128],
                            act_w(kk, w),
                            start=(kk == 0), stop=(kk == NF - 1),
                            inc=mm if kk == NF - 1 else None)
                    s.wait(mq)
                    if _os.environ.get("SKIP_E"):
                        continue
                    e1 = s.op("activation",
                              out=QRW[:, (part * 2 + tp) * 256:
                                      (part * 2 + tp + 1) * 256],
                              in_=bank[:, 0:256],
                              func=AF.Copy, inc=sc)
                    e2 = s.op("activation",
                              out=b.sc2[:, 1024 + (part * 2 + tp) * 256:
                                        1024 + (part * 2 + tp + 1) * 256],
                              in_=bank[:, 0:256],
                              func=AF.Square, inc=sc)
                    prev[("qke", part, tp)] = e2
            for vt in (() if _os.environ.get("SKIP_V") else range(2)):
                bank = [b.pA, b.pB][vt]
                t.wait(tok_w)
                t.wait(frz)
                t.wait(prev.get(("vread", ring)))
                t.wait(prev.get(("qke", 0, vt)))
                mv = None
                for kk in range(NF):
                    mv = t.op("matmul", bank[:, 0:256],
                              b.act[:, kk * S + w * 256 + vt * 128:
                                    kk * S + w * 256 + (vt + 1) * 128],
                              WQK[:, kk * 768 + 512: kk * 768 + 768],
                              start=(kk == 0), stop=(kk == NF - 1),
                              inc=mm if kk == NF - 1 else None)
                vdst = VRG[:, (ring * 2 + vt) * 260:(ring * 2 + vt + 1) * 260]
                g.wait(mv)
                g.wait(prev.get(("vread", ring)))
                tok_vo = g.op("memset",
                              vdst.rearrange("p (h c) -> p h c", c=65)[:, :, 64:65],
                              1.0, inc=gp)
                v.wait(mv)
                v.wait(tok_vo)
                prev[("ve_", ring, vt)] = v.op(
                    "tensor_copy",
                    out=vdst.rearrange("p (h c) -> p h c", c=65)[:, :, 0:64],
                    in_=bank[:, 0:256].rearrange("p (h c) -> p h c", c=64),
                    inc=ve)
                prev[("vev", vt)] = prev[("ve_", ring, vt)]
            if STAGE < 3:
                continue
            t.wait(prev[("qke", 1, 1)])
            t.wait(prev.get("nrm_free"))
            t.wait(prev.get("pb_free"))
            mtn = None
            for ts2 in range(2):
                for part in range(2):
                    for tp in range(2):
                        col = (ts2 * 8 + part * 4 + tp * 2) * 4
                        mtn = t.op(
                            "matmul", b.pW[:, col:col + 2],
                            b.sc2[:, 1024 + (part * 2 + tp) * 256 + ts2 * 128:
                                  1024 + (part * 2 + tp) * 256 + (ts2 + 1) * 128],
                            b.cb["c_ones2"][:, 0:2], start=True, stop=True,
                            inc=mm if (ts2, part, tp) == (1, 1, 1) else None)
            v.wait(mtn)
            v.opw("tensor_copy", sem=ve,
                  out=b.cols[:, 64:80].rearrange("p (a o) -> p a o", o=2),
                  in_=b.pW[:, 0:64].rearrange("p (a o) -> p a o", o=8)[:, :, 0:2])
            prev["nrm_free"] = (ve, ve.count)
            v.opw("tensor_single_scalar", sem=ve, out=b.cols[:, 64:80],
                  in_=b.cols[:, 64:80], scalar=b.cb["c_eps24"][:], op=ALU.max)
            s.wait((ve, ve.count))
            s.opw("activation", sem=sc, out=b.cols[:, 64:80],
                  in_=b.cols[:, 64:80], func=AF.Ln)
            tok_inv = s.op("activation", out=b.cols[:, 80:96],
                           in_=b.cols[:, 64:80], func=AF.Exp, scale=-0.5, inc=sc)
            v.wait(tok_inv)
            tok_ivc = v.op("tensor_copy", out=b.cols[:, 96:112],
                           in_=b.cols[:, 80:96], inc=ve)
            t.wait(tok_ivc)
            t.wait(prev.get("ivr_free"))
            ttr = None
            for ts2 in range(2):
                ttr = t.op("transpose", b.pV[0:8, ts2 * 128:(ts2 + 1) * 128],
                           b.cols[:, 96 + ts2 * 8: 96 + (ts2 + 1) * 8],
                           b.ident[:], inc=mm)
            v.wait(ttr)
            tok_ivr = v.op("tensor_copy", out=IVR[0:8, 0:256],
                           in_=b.pV[0:8, 0:256], inc=ve)
            prev["ivr_free"] = tok_ivr
            if STAGE < 4:
                continue
            rot_last = None
            for part in range(2):
                scol = qs_col if part == 0 else ks_col
                csoff = 0 if part == 0 else 512
                for tp in range(2):
                    t.wait(tok_ivr)
                    t.wait(prev.get("bc_free"))
                    vv = part * 2 + tp
                    tb = t.op("matmul", b.pC[:, 256:512],
                              b.cb["c_sel8"][0:8, vv * 128:(vv + 1) * 128],
                              IVR[0:8, 0:256], start=True, stop=True, inc=mm)
                    v.wait(tb)
                    v.wait(tok_rot)
                    v.wait(rot_last)
                    v.wait(prev.get(("kread", ring)) if part == 1 else None)
                    v.wait(prev.get("qread"))
                    srcq = QRW[:, (part * 2 + tp) * 256:(part * 2 + tp + 1) * 256]
                    v.opw("scalar_tensor_tensor", sem=ve, out=b.tl[:, 0:256],
                          in0=srcq, scalar=scol, in1=b.pC[:, 256:512],
                          op0=ALU.mult, op1=ALU.mult)
                    prev["bc_free"] = (ve, ve.count)
                    v.opw("tensor_copy", sem=ve, out=srcq, in_=b.tl[:, 0:256])
                    v.opw("tensor_tensor", sem=ve, out=b.tl2[:, 0:256],
                          in0=b.tl[:, 0:256],
                          in1=b.sc2[:, csoff:csoff + 256], op=ALU.mult)
                    t.wait((ve, ve.count))
                    t.wait(prev.get("j_free"))
                    tj = t.op("matmul", b.pD[:, 256:512], b.cb["c_J"][:],
                              srcq, start=True, stop=True, inc=mm)
                    v.wait(tj)
                    v.opw("tensor_tensor", sem=ve, out=b.tl[:, 0:256],
                          in0=b.pD[:, 256:512],
                          in1=b.sc2[:, csoff + 256: csoff + 512], op=ALU.mult)
                    prev["j_free"] = (ve, ve.count)
                    if part == 0:
                        d0 = QW[0:64, (2 * tp) * 256:(2 * tp + 1) * 256]
                        d1 = QW[0:64, (2 * tp + 1) * 256:(2 * tp + 2) * 256]
                    else:
                        bas = ring * 1024
                        d0 = KRG[0:64, bas + (2 * tp) * 256: bas + (2 * tp + 1) * 256]
                        d1 = KRG[0:64, bas + (2 * tp + 1) * 256: bas + (2 * tp + 2) * 256]
                    v.op("tensor_add", out=d0, in0=b.tl[0:64, 0:256],
                         in1=b.tl2[0:64, 0:256])
                    rot_last = v.op("tensor_add", out=d1, in0=b.tl[64:128, 0:256],
                                    in1=b.tl2[64:128, 0:256], inc=ve)
            prev["rot_free"] = rot_last
            if STAGE < 5:
                continue
            l_cols = b.cols[:, 112:120]
            nm_cols = b.cols[:, 120:128]
            tok_lc = None
            pa_mm_last = None
            for hh in range(4):
                for qs2 in range(2):
                    t.wait(rot_last)
                    t.wait(prev.get("pa_free"))
                    ksubs = list(range(2, 4)) if w == 0 else list(range(4))
                    ma = None
                    for i, ks2 in enumerate(ksubs):
                        rsel = (1 - ring) if ks2 < 2 else ring
                        koff = rsel * 1024 + hh * 256 + (ks2 % 2) * 128
                        ma = t.op(
                            "matmul", b.pV[:, 256 + i * 128: 256 + (i + 1) * 128],
                            QW[0:64, hh * 256 + qs2 * 128:
                               hh * 256 + (qs2 + 1) * 128],
                            KRG[0:64, koff: koff + 128],
                            start=True, stop=True,
                            inc=mm if i == len(ksubs) - 1 else None)
                    pa_mm_last = ma
                    nks = len(ksubs) * 128
                    mvar = qs2 if w > 0 else (2 + qs2)
                    v.wait(ma)
                    v.wait(prev.get("pax"))
                    v.opw("tensor_tensor", sem=ve, out=MSB[:, 0:nks],
                          in0=b.pV[:, 256:256 + nks],
                          in1=b.cb["c_maskA"][:, mvar * 512: mvar * 512 + nks],
                          op=ALU.add)
                    prev["pa_free"] = (ve, ve.count)
                    tok_nm = v.op("reduce_max",
                                  out=nm_cols[:, hh * 2 + qs2: hh * 2 + qs2 + 1],
                                  in_=MSB[:, 0:nks], axis=AX.X, negate=True,
                                  inc=ve)
                    s.wait(tok_nm)
                    tok_lc = s.op(
                        "activation", out=SC2[:, 0:nks], in_=MSB[:, 0:nks],
                        func=AF.Exp,
                        bias=nm_cols[:, hh * 2 + qs2: hh * 2 + qs2 + 1],
                        accum_out=l_cols[:, hh * 2 + qs2: hh * 2 + qs2 + 1],
                        inc=sc)
                    prev["pax"] = tok_lc
            if STAGE < 6:
                continue
            s.wait(tok_lc)
            s.opw("activation", sem=sc, out=MCH[:, 0:8], in_=l_cols, func=AF.Ln)
            v.wait((sc, sc.count))
            v.opw("tensor_sub", sem=ve, out=MCH[:, 8:16], in0=nm_cols,
                  in1=MCH[:, 0:8])
            v.opw("tensor_copy", sem=ve, out=b.colsbf[:, 16:24], in_=MCH[:, 8:16])
            v.opw("tensor_copy", sem=ve, out=MCH[:, 16:24], in_=b.colsbf[:, 16:24])
            v.opw("tensor_sub", sem=ve, out=MCH[:, 24:32], in0=MCH[:, 8:16],
                  in1=MCH[:, 16:24])
            v.opw("tensor_copy", sem=ve,
                  out=MCH[:, 32:48].rearrange("p (q l h) -> p q l h", q=2, l=2),
                  in_=MCH[:, 16:32].rearrange("p (l h q) -> p q l h", l=2, q=2))
            t.wait((ve, ve.count))
            t.wait(prev.get("mr_free"))
            ttr3 = None
            for qs2 in range(2):
                ttr3 = t.op("transpose", b.pV[0:8, qs2 * 128:(qs2 + 1) * 128],
                            MCH[:, 32 + qs2 * 8: 32 + (qs2 + 1) * 8],
                            b.ident[:], inc=mm)
            v.wait(ttr3)
            tok_mr = v.op("tensor_copy", out=MRW[0:8, 0:256],
                          in_=b.pV[0:8, 0:256], inc=ve)
            prev["mr_free"] = tok_mr
            v.wait(tok_mr)
            tok_mrb = v.op("tensor_copy", out=b.rows32bf[0:8, 0:256],
                           in_=MRW[0:8, 0:256], inc=ve)
            if STAGE < 7:
                continue
            kread_last = None
            pv_last = None
            for hh in range(4):
                t.wait(tok_mrb)
                t.wait(prev.get("pb_free"))
                ksubs = list(range(2, 4)) if w == 0 else list(range(4))
                mb = None
                for i, ks2 in enumerate(ksubs):
                    rsel = (1 - ring) if ks2 < 2 else ring
                    koff = rsel * 1024 + hh * 256 + (ks2 % 2) * 128
                    t.op("matmul", b.pW[:, i * 256:(i + 1) * 256],
                         KRG[0:64, koff: koff + 128],
                         QW[0:64, hh * 256:(hh + 1) * 256],
                         start=True, stop=False)
                    mb = t.op("matmul", b.pW[:, i * 256:(i + 1) * 256],
                              b.cb["c_sn8"][0:8, hh * 128:(hh + 1) * 128],
                              b.rows32bf[0:8, 0:256], start=False, stop=True,
                              inc=mm if i == len(ksubs) - 1 else None)
                kread_last = mb
                nks = len(ksubs)
                v.wait(mb)
                v.wait(prev.get("pbx"))
                mtt = None
                for i, ks2 in enumerate(ksubs):
                    mbi = ks2 if w > 0 else (4 + (ks2 - 2))
                    mtt = v.op("tensor_tensor",
                               out=MSB[:, i * 256:(i + 1) * 256],
                               in0=b.pW[:, i * 256:(i + 1) * 256],
                               in1=b.cb["c_maskB"][:, mbi * 256:(mbi + 1) * 256],
                               op=ALU.add, inc=ve if i == nks - 1 else None)
                prev["pb_free"] = mtt
                s.wait(mtt)
                s.wait(prev.get("psb_free"))
                etok = s.op("activation", out=PSB[:, 0:nks * 256],
                            in_=MSB[:, 0:nks * 256], func=AF.Exp, inc=sc)
                prev["pbx"] = etok
                t.wait(etok)
                t.wait(prev.get("pv_roll"))
                mpv = None
                for i, ks2 in enumerate(ksubs):
                    rsel = (1 - ring) if ks2 < 2 else ring
                    vt = ks2 % 2
                    vsl = VRG[:, (rsel * 2 + vt) * 260 + hh * 65:
                              (rsel * 2 + vt) * 260 + hh * 65 + 65]
                    mpv = t.op("matmul", b.pV[0:65, 512:768],
                               vsl, PSB[:, i * 256:(i + 1) * 256],
                               start=(i == 0), stop=(i == nks - 1),
                               inc=mm if i == nks - 1 else None)
                pv_last = mpv
                prev["psb_free"] = mpv
                v.wait(mpv)
                tp = hh // 2
                rr2 = (hh % 2) * 64
                prev[("pv_free", hh)] = v.op(
                    "tensor_copy",
                    out=AOW[rr2:rr2 + 64, tp * 256:(tp + 1) * 256],
                    in_=b.pV[0:64, 512:768], inc=ve)
                prev["pv_roll"] = prev[("pv_free", hh)]
            prev[("vread", ring)] = pv_last
            prev[("vread", 1 - ring)] = pv_last
            prev[("kread", ring)] = kread_last
            prev[("kread", 1 - ring)] = kread_last
            prev["qread"] = kread_last
            if STAGE < 8:
                continue
            tok_aow = prev[("pv_free", 3)]
            for dc in range(NF):
                bank = [b.pC, b.pD][dc % 2]
                t.wait(tok_aow)
                t.wait(prev[("pv_free", dc % 4)])
                t.wait(tok_wo)
                t.wait(prev.get(("wo_acc", (dc - 2) % NF)))
                mo = None
                for r2 in range(2):
                    mo = t.op("matmul", bank[:, 0:256],
                              WOR[:, r2 * 1024 + dc * 128:
                                  r2 * 1024 + (dc + 1) * 128],
                              AOW[:, r2 * 256:(r2 + 1) * 256],
                              start=(r2 == 0), stop=(r2 == 1),
                              inc=mm if r2 == 1 else None)
                v.wait(mo)
                prev[("wo_acc", dc)] = v.op(
                    "tensor_add", out=xt(b, dc, w * 256, 256),
                    in0=xt(b, dc, w * 256, 256), in1=bank[:, 0:256], inc=ve)
        if ("qke", 1, 1) in prev:
            prev["wqk_free"] = prev[("qke", 1, 1)]
        if STAGE >= 8:
            prev["wor_free"] = prev[("wo_acc", NF - 1)]
        if STAGE >= 4:
            prev["rot_free"] = prev["j_free"]
    prev["act_free"] = prev.get(("wo_acc", NF - 1),
                                prev.get(("qke", 1, 1), prev["frz"]))
    b.inner_prev = prev
    return prev["act_free"]


def emit_wspill(bld, wsrc, wdst, nrow, ncol, gate=None):
    b = bld
    g, y = b.g, b.y
    g.wait(gate)
    y.wait(gate)
    CW = 2048
    SP = lambda sl: A(b, 20000 + sl * 2048, 128, 2048)
    prev_sp = [gate, gate]
    tok = None
    si = 0
    for rt in range((nrow + 127) // 128):
        pr = min(128, nrow - rt * 128)
        for c0 in range(0, ncol, CW):
            n = min(CW, ncol - c0)
            sl = si % 2
            si += 1
            g.wait(prev_sp[sl])
            ti = g.dma("dg%d" % sl, out=SP(sl)[0:pr, 0:n],
                       in_=wsrc[rt * 128: rt * 128 + pr, c0:c0 + n],
                       first=True)
            y.wait(ti)
            tok = y.dma("dy%d" % sl, out=wdst[rt * 128: rt * 128 + pr, c0:c0 + n],
                        in_=SP(sl)[0:pr, 0:n], first=True)
            prev_sp[sl] = tok
    return [prev_sp[0], prev_sp[1]]


def emit_inner_ffn(bld, lay, w1bf, w2bf, g_cols, bln_cols, b1a_cols, b1g_cols,
                   b2_cols, tok_w_ready):
    b = bld
    t, v, s, g, y = b.t, b.v, b.s, b.g, b.y
    mm, ve, sc, gp = b.sm["mm"], b.sm["ve"], b.sm["sc"], b.sm["gp"]
    prev = b.inner_prev
    for st in (t, v, s, g):
        st.wait(prev.get("act_free"))
    tok_xln = emit_ln_feature(
        b, src_fn=lambda j, c0, n: b.xf[:, j * TW + c0 + (1 if j < NF // 2
                                        else 0): j * TW + c0 + n +
                                        (1 if j < NF // 2 else 0)],
        dst_fn=lambda j, c0, n: b.act[:, j * S + c0: j * S + c0 + n],
        dst_dtype=BF16, gcol=g_cols, bcol=bln_cols, tag="iln%d" % lay)
    act_tile = lambda j, c0, n: b.act[:, j * S + c0: j * S + c0 + n]
    HH = A(b, 0, 128, 11264)
    W1B = lambda sl: A(b, 11264 + sl * 4096, 128, 4096)
    W2B = lambda sl: A(b, 19456 + sl * 1024, 128, 1024)
    GSC = A(b, 21504, 128, 2048)
    FCW = [256] * 10 + [170]
    psl8 = [(b.pA, 0), (b.pB, 0), (b.pC, 0), (b.pD, 0),
            (b.pW, 0), (b.pW, 512), (b.pV, 0), (b.pV, 512)]
    ft_tiles = []
    f = 0
    while f < DIN:
        ft_tiles.append((f, min(128, DIN - f)))
        f += 128
    prev_w1 = [tok_w_ready, tok_w_ready]
    prev_w2 = [tok_w_ready, tok_w_ready]
    prev_hh = [None] * 22
    prev_acc = [None] * NF
    prev_w2mm = [None, None]
    hh_free = None
    for tokc in range(4):
        c0t = tokc * 512
        for fc in range(11):
            fw = FCW[fc]
            f0 = fc * 256
            nm_ = (fw + 127) // 128
            sl = fc % 2
            g.wait(prev_w1[sl])
            tw1 = None
            for kk in range(NF):
                tw1 = g.dma("dg%d" % sl,
                            out=W1B(sl)[:, kk * 512: kk * 512 + fw],
                            in_=w1bf[kk * 128:(kk + 1) * 128, f0:f0 + fw],
                            first=(kk == 0))
                tw1 = g.dma("dg%d" % sl,
                            out=W1B(sl)[:, kk * 512 + 256: kk * 512 + 256 + fw],
                            in_=w1bf[kk * 128:(kk + 1) * 128,
                                     DIN + f0: DIN + f0 + fw])
            banks = [b.pA, b.pB, b.pC, b.pD]
            mms = []
            for part in range(2):
                for m in range(nm_):
                    mw = min(128, fw - m * 128)
                    bank = banks[part * 2 + m]
                    t.wait(tw1)
                    t.wait(tok_xln)
                    t.wait(prev.get(("w1p", part, m)))
                    t.wait(prev_acc[part * 2 + m])
                    mq = None
                    for kk in range(NF):
                        mq = t.op(
                            "matmul", bank[0:mw, 0:512],
                            W1B(sl)[:, kk * 512 + part * 256 + m * 128:
                                    kk * 512 + part * 256 + m * 128 + mw],
                            act_tile(kk, c0t, 512),
                            start=(kk == 0), stop=(kk == NF - 1),
                            inc=mm if kk == NF - 1 else None)
                    mms.append((part, m, mw, bank, mq))
            prev_w1[sl] = mms[-1][4]
            for part, m, mw, bank, mq in mms:
                ft = fc * 2 + m
                s.wait(mq)
                if part == 1:
                    prev[("w1p", part, m)] = s.op(
                        "activation", out=GSC[0:mw, m * 1024: m * 1024 + 512],
                        in_=bank[0:mw, 0:512], func=AF.Gelu,
                        bias=b1g_cols[0:mw, ft:ft + 1], inc=sc)
                else:
                    v.wait(mq)
                    v.wait(prev_hh[((fc - 1) % 11) * 2 + m])
                    prev[("w1p", part, m)] = v.op(
                        "tensor_scalar_add",
                        out=GSC[0:mw, m * 1024 + 512:(m + 1) * 1024],
                        in0=bank[0:mw, 0:512],
                        scalar1=b1a_cols[0:mw, ft:ft + 1], inc=ve)
            for m in range(nm_):
                mw = min(128, FCW[fc] - m * 128)
                ft = fc * 2 + m
                v.wait(prev[("w1p", 0, m)])
                v.wait(prev[("w1p", 1, m)])
                v.wait(prev_hh[ft])
                v.wait(hh_free)
                prev_hh[ft] = v.op("tensor_mul",
                                   out=HH[0:mw, ft * 512:(ft + 1) * 512],
                                   in0=GSC[0:mw, m * 1024 + 512:(m + 1) * 1024],
                                   in1=GSC[0:mw, m * 1024: m * 1024 + 512],
                                   inc=ve)
                s.wait(prev_hh[ft])
        for part in range(2):
            for m in range(2):
                t.wait(prev[("w1p", part, m)])
        mo = None
        for fi, (f0_, kw) in enumerate(ft_tiles):
            sl = fi % 2
            g.wait(prev_w2[sl])
            g.wait(prev_w2mm[sl])
            tw2 = g.dma("dg%d" % sl, out=W2B(sl)[0:kw, :],
                        in_=w2bf[f0_:f0_ + kw, :], first=True)
            prev_w2[sl] = tw2
            t.wait(tw2)
            t.wait(prev_hh[fi])
            if fi == 0:
                for dc in range(NF):
                    t.wait(prev_acc[dc])
            for dc in range(NF):
                bank, boff = psl8[dc]
                mo = t.op("matmul", bank[:, boff:boff + 512],
                          W2B(sl)[0:kw, dc * 128:(dc + 1) * 128],
                          HH[0:kw, fi * 512:(fi + 1) * 512],
                          start=(fi == 0), stop=(fi == len(ft_tiles) - 1),
                          skip_group_check=True,
                          inc=mm if dc == NF - 1 else None)
            prev_w2mm[sl] = mo
        hh_free = (mm, mm.count)
        for dc in range(NF):
            bank, boff = psl8[dc]
            v.wait(mo)
            prev_acc[dc] = v.op(
                "scalar_tensor_tensor", out=xt(b, dc, c0t, 512),
                in0=bank[:, boff:boff + 512], scalar=b2_cols[:, dc:dc + 1],
                in1=xt(b, dc, c0t, 512), op0=ALU.add, op1=ALU.add, inc=ve)
    prev["act_free"] = prev_acc[NF - 1]
    b.inner_prev = prev
    return prev_acc[NF - 1]


def emit_l1_ffn_sandwich(bld, wqkv6, wo6, w1f6, w2f6, w1bf, w2bf, xspill,
                         pcols):
    import os as _fs_os
    """x += LN(inner_stack_6(LN(x))). wqkv6 etc: HBM aps (full 6-layer).
    w1bf/w2bf: bf16 HBM scratch (one layer at a time). xspill: HBM f32
    [1024, 2048] scratch. pcols: dict of col APs."""
    b = bld
    t, v, s, g, y = b.t, b.v, b.s, b.g, b.y
    tok_sp = None
    for j in range(NF):
        y.wait(b.inner_prev.get("any"))
        tok_sp = y.dma("dy1", out=xspill[j * 128:(j + 1) * 128, :],
                       in_=xt(b, j), first=(j == 0))
    for st in (t, v, s, g):
        st.wait(tok_sp)
    tok_u = emit_ln_feature(
        b, src_fn=lambda j, c0, n: xt(b, j, c0, n),
        dst_fn=lambda j, c0, n: xt(b, j, c0, n),
        dst_dtype=F32, tag="l1fln1")
    b.inner_prev = {"act_free": tok_u}
    nlay = int(_fs_os.environ.get("L1F_NL", str(DEPTH_IN)))
    for lay in range(nlay):
        gate = b.inner_prev.get("act_free")
        tok_w1 = emit_wspill(b, w1f6[lay], w1bf, 1024, 5460, gate=gate)
        tok_w2 = emit_wspill(b, w2f6[lay], w2bf, 2730, 1024, gate=tok_w1)
        b.inner_prev["wqk_free"] = tok_w2
        b.inner_prev["wor_free"] = tok_w2
        emit_inner_attn(b, lay, wqkv6[lay], wo6[lay],
                        pcols["qs"][:, lay:lay + 1], pcols["ks"][:, lay:lay + 1])
        emit_inner_ffn(b, lay, w1bf, w2bf,
                       [pcols["g"][:, lay * NF + j: lay * NF + j + 1]
                        for j in range(NF)],
                       [pcols["bln"][:, lay * NF + j: lay * NF + j + 1]
                        for j in range(NF)],
                       pcols["b1a"][:, lay * 22: (lay + 1) * 22],
                       pcols["b1g"][:, lay * 22: (lay + 1) * 22],
                       pcols["b2"][:, lay * NF: (lay + 1) * NF], tok_w2)
    prev = b.inner_prev
    for st in (t, v, s, g, y):
        st.wait(prev["act_free"])
    tok_fin = emit_ln_feature(
        b, src_fn=lambda j, c0, n: xt(b, j, c0, n),
        dst_fn=None, dst_dtype=None, reload_hbm=xspill, tag="l1fln2")
    return tok_fin


from contextlib import ExitStack as _ExitStack

_BUILD_CACHE = {}


def _std_prologue(bld, extra_toks=()):
    for st in (bld.v, bld.t, bld.s, bld.g):
        st.wait(bld.tok_x)
        st.wait(bld.tok_guard)
        st.wait(bld.tok_consts)
        for tk in extra_toks:
            st.wait(tk)


def _build_l0a():
    ctx = _ExitStack()
    bld = Build(ctx, [])
    alloc_bufs(bld)
    emit_consts(bld)
    xin = bld.inp("x_in", [1024, 2048], F32)
    wq = bld.inp("wq", [1024, 1024], F32)
    wk = bld.inp("wk", [1024, 1024], F32)
    wv = bld.inp("wv", [1024, 1024], F32)
    wo = bld.inp("wo", [1024, 1024], F32)
    xout = bld.outp("x_out", [1024, 2048], F32)
    emit_load_x(bld, xin)
    _std_prologue(bld)
    tok = emit_l0_attn_sandwich(bld, wq, wk, wv, wo)
    emit_store_x(bld, xout, tok)
    bld.finalize()
    return bld


def _build_l0f():
    ctx = _ExitStack()
    bld = Build(ctx, [])
    alloc_bufs(bld)
    emit_consts(bld)
    xin = bld.inp("x_in", [1024, 2048], F32)
    w1h = bld.inp("w1", [1024, 4096], F32)
    w2h = bld.inp("w2", [4096, 1024], F32)
    b1h = bld.inp("b1c", [128, 32], F32)
    b2h = bld.inp("b2c", [128, 8], F32)
    w1t = bld.dram("w1t", [1024, 4096], BF16)
    w2t = bld.dram("w2t", [4096, 1024], BF16)
    xout = bld.outp("x_out", [1024, 2048], F32)
    pc = bld.pcol
    bld.y.dma("dyc", out=pc[:, 0:32], in_=b1h)
    tok_p = bld.y.dma("dyc", out=pc[:, 32:40], in_=b2h)
    emit_load_x(bld, xin)
    _std_prologue(bld, (tok_p,))
    tok = emit_l0_ffn_sandwich(bld, w1h, w2h, pc[:, 0:32], pc[:, 32:40],
                               w1t, w2t, tok_p)
    emit_store_x(bld, xout, tok)
    bld.finalize()
    return bld


def _build_l1a():
    ctx = _ExitStack()
    bld = Build(ctx, [])
    alloc_bufs(bld)
    emit_consts(bld)
    xin = bld.inp("x_in", [1024, 2048], F32)
    wh = bld.inp("wqkv", [1024, 3072], F32)
    bqh = bld.inp("bq_col", [128, 8], F32)
    bkvh = bld.inp("bkv_row", [1, 3072], F32)
    xout = bld.outp("x_out", [1024, 2048], F32)
    pc = bld.pcol
    tok_p = bld.y.dma("dyc", out=pc[:, 0:8], in_=bqh)
    emit_load_x(bld, xin)
    hi, lo, tok_bkv = emit_bkv_prep(bld, bkvh)
    _std_prologue(bld, (tok_p,))
    tok = emit_l1_attn_sandwich(bld, wh, pc[:, 0:8], (hi, lo, tok_bkv))
    emit_store_x(bld, xout, tok)
    bld.finalize()
    return bld


def _build_l1f():
    ctx = _ExitStack()
    bld = Build(ctx, [])
    alloc_bufs(bld)
    emit_consts(bld)
    xin = bld.inp("x_in", [1024, 2048], F32)
    wqkv6 = [bld.inp("wqkv%d" % i, [1024, 3072], F32) for i in range(6)]
    wo6 = [bld.inp("wo%d" % i, [1024, 1024], F32) for i in range(6)]
    w1f6 = [bld.inp("w1_%d" % i, [1024, 5460], F32) for i in range(6)]
    w2f6 = [bld.inp("w2_%d" % i, [2730, 1024], F32) for i in range(6)]
    pc_h = bld.inp("pc", [128, 420], F32)
    xout = bld.outp("x_out", [1024, 2048], F32)
    w1bf = bld.dram("w1bf", [1024, 5460], BF16)
    w2bf = bld.dram("w2bf", [2730, 1024], BF16)
    xspill = bld.dram("xspill", [1024, 2048], F32)
    pc = bld.pcol
    tok_p = bld.y.dma("dyc", out=pc[:, 0:420], in_=pc_h)
    emit_load_x(bld, xin)
    _std_prologue(bld, (tok_p,))
    bld.y.wait(bld.tok_x)
    bld.inner_prev = {}
    pcols = {"qs": pc[:, 0:6], "ks": pc[:, 6:12], "g": pc[:, 12:60],
             "bln": pc[:, 60:108], "b1a": pc[:, 108:240],
             "b1g": pc[:, 240:372], "b2": pc[:, 372:420]}
    tok = emit_l1_ffn_sandwich(bld, wqkv6, wo6, w1f6, w2f6, w1bf, w2bf,
                               xspill, pcols)
    emit_store_x(bld, xout, tok)
    bld.finalize()
    return bld


def _get_build(name):
    if name not in _BUILD_CACHE:
        _BUILD_CACHE[name] = {
            "l0a": _build_l0a, "l0f": _build_l0f,
            "l1a": _build_l1a, "l1f": _build_l1f}[name]()
    return _BUILD_CACHE[name]


def _cols_of(vec, n):
    v2 = np.zeros(n * 128, np.float32)
    v2[:len(vec)] = vec
    return np.ascontiguousarray(v2.reshape(n, 128).T)


def _launch(bld, shared, xs):
    from concourse.bass_utils import run_bass_kernel_spmd
    maps = []
    for b in range(8):
        m = dict(shared)
        m["x_in"] = xs[b]
        maps.append(m)
    res = run_bass_kernel_spmd(bld.nc, maps, core_ids=list(range(8)))
    return [np.asarray(res.results[b]["x_out"]) for b in range(8)]


def kernel(**inputs):
    inp = {k: np.asarray(v, dtype=np.float32) for k, v in inputs.items()}
    x = inp["L0_attn_X"]
    consts = const_np_inputs()

    xs = [np.ascontiguousarray(x[b].T) for b in range(8)]

    sh = dict(consts)
    sh["wq"] = np.ascontiguousarray(inp["L0_attn_q_proj"].T)
    sh["wk"] = np.ascontiguousarray(inp["L0_attn_k_proj"].T)
    sh["wv"] = np.ascontiguousarray(inp["L0_attn_v_proj"].T)
    sh["wo"] = np.ascontiguousarray(inp["L0_attn_o_proj"].T)
    xs = _launch(_get_build("l0a"), sh, xs)

    sh = dict(consts)
    sh["w1"] = np.ascontiguousarray(inp["L0_ffn_W1"].T)
    sh["w2"] = np.ascontiguousarray(inp["L0_ffn_W2"].T)
    sh["b1c"] = _cols_of(inp["L0_ffn_b1"], 32)
    sh["b2c"] = _cols_of(inp["L0_ffn_b2"], 8)
    xs = _launch(_get_build("l0f"), sh, xs)

    sh = dict(consts)
    sh["wqkv"] = np.ascontiguousarray(inp["L1_attn_W_qkv"])
    bqkv = inp["L1_attn_b_qkv"]
    sh["bq_col"] = _cols_of(bqkv[0:1024], 8)
    sh["bkv_row"] = np.ascontiguousarray(bqkv.reshape(1, 3072))
    xs = _launch(_get_build("l1a"), sh, xs)

    sh = dict(consts)
    for i in range(6):
        sh["wqkv%d" % i] = np.ascontiguousarray(inp["L1_ffn_Wqkv"][i])
        sh["wo%d" % i] = np.ascontiguousarray(inp["L1_ffn_Wo"][i])
        sh["w1_%d" % i] = np.ascontiguousarray(inp["L1_ffn_W1"][i])
        sh["w2_%d" % i] = np.ascontiguousarray(inp["L1_ffn_W2"][i])
    PC = np.zeros((128, 420), np.float32)
    for lay in range(6):
        PC[:, lay] = np.tile(inp["L1_ffn_q_scale"][lay], 2)
        PC[:, 6 + lay] = np.tile(inp["L1_ffn_k_scale"][lay], 2)
        PC[:, 12 + lay * 8: 12 + (lay + 1) * 8] = _cols_of(
            inp["L1_ffn_ln_g"][lay], 8)
        PC[:, 60 + lay * 8: 60 + (lay + 1) * 8] = _cols_of(
            inp["L1_ffn_ln_b"][lay], 8)
        PC[:, 108 + lay * 22: 108 + (lay + 1) * 22] = _cols_of(
            inp["L1_ffn_b1"][lay][:2730], 22)
        PC[:, 240 + lay * 22: 240 + (lay + 1) * 22] = _cols_of(
            inp["L1_ffn_b1"][lay][2730:], 22)
        PC[:, 372 + lay * 8: 372 + (lay + 1) * 8] = _cols_of(
            inp["L1_ffn_b2"][lay], 8)
    sh["pc"] = PC
    xs = _launch(_get_build("l1f"), sh, xs)

    out = np.empty((8, 2048, 1024), np.float32)
    for b in range(8):
        out[b] = xs[b].T
    return out
